# version 09
# baseline (speedup 1.0000x reference)
import sys

sys.path.insert(0, "/opt/trn_rl_repo")

import numpy as np
import ml_dtypes

import concourse.bass as bass
import concourse.mybir as mybir
from concourse.bass_utils import run_bass_kernel_spmd
import concourse.bass_utils as _bass_utils

_orig_run_command = _bass_utils.run_command


def _patched_run_command(cmd, **kw):
    if isinstance(cmd, list):
        cmd = [("--enable-ldw-opt=true" if c == "--enable-ldw-opt=false" else c)
               for c in cmd]
    return _orig_run_command(cmd, **kw)


_bass_utils.run_command = _patched_run_command

F32 = mybir.dt.float32
BF16 = mybir.dt.bfloat16
I32 = mybir.dt.int32
AF = mybir.ActivationFunctionType
ALU = mybir.AluOpType

H, D, BS = 32, 64, 256
DM = 2048
HID = 5632
T = 512
NB = 2
EPS = 1e-5
KC = DM // 128
MC_QKV = 3 * DM // 128
MC_G = HID // 128
TC = T // 128
NCORE = 8
LAYER_IDX, NUM_LAYERS = 0, 32


class Sem:
    __slots__ = ("name", "h", "n")

    def __init__(self, name):
        self.name, self.h, self.n = name, None, 0


class Walk:

    def __init__(self, who, eng, sems):
        self.who = who
        self.eng = eng
        for s in sems:
            s.n = 0
        self.last_wait = {}

    def on(self, name):
        return self.who == name

    def wait(self, who, sem, val=None):
        v = sem.n if val is None else val
        if self.who == who and v > 0:
            if v > self.last_wait.get(sem.name, 0):
                self.eng.wait_ge(sem.h, v)
                self.last_wait[sem.name] = v

    def emit(self, who, build, incs=()):
        ins = build(self.eng) if self.who == who else None
        for i, (sem, k) in enumerate(incs):
            sem.n += k
            if ins is not None:
                if i == 0:
                    ins.then_inc(sem.h, k)
                else:
                    self.eng.nop().then_inc(sem.h, k)
        return ins

    def mark(self, who, sem, k=1):
        sem.n += k
        if self.who == who:
            self.eng.nop().then_inc(sem.h, k)


class Ring:
    def __init__(self, nslots):
        self.n = nslots
        self.free = [None] * nslots

    def acquire(self, w, who, idx):
        evt = self.free[idx % self.n]
        if evt is not None:
            for sem, val in evt:
                w.wait(who, sem, val)

    def release(self, idx, sem, val=None):
        self.free[idx % self.n] = [(sem, sem.n if val is None else val)]

    def release_multi(self, idx, evs):
        self.free[idx % self.n] = [e for e in evs if e is not None]


def build_program(n_tiles=8, stop_after=None, debug_taps=()):
    nc = bass.Bass("TRN2", target_bir_lowering=False)

    xin = nc.dram_tensor("xin", [8 * T, DM], F32, kind="ExternalInput").ap()
    qkvT = nc.dram_tensor("qkvT", [MC_QKV * 128, 2048], F32, kind="ExternalInput").ap()
    gateT = nc.dram_tensor("gateT", [KC * 128, 2048], F32, kind="ExternalInput").ap()
    outT = nc.dram_tensor("outT", [16 * 128, 2048], F32, kind="ExternalInput").ap()
    wqgT = nc.dram_tensor("wqgT", [MC_G * 128, 2048], I32, kind="ExternalInput").ap()
    wquT = nc.dram_tensor("wquT", [MC_G * 128, 2048], I32, kind="ExternalInput").ap()
    wqdT = nc.dram_tensor("wqdT", [44 * 128, 2048], I32, kind="ExternalInput").ap()
    sg_in = nc.dram_tensor("sg_p", [128, MC_G], F32, kind="ExternalInput").ap()
    su_in = nc.dram_tensor("su_p", [128, MC_G], F32, kind="ExternalInput").ap()
    sd_in = nc.dram_tensor("sd_bc", [128, DM], F32, kind="ExternalInput").ap()
    nw_in = nc.dram_tensor("nw_p", [128, KC], F32, kind="ExternalInput").ap()
    qd_in = nc.dram_tensor("qd_t", [128, KC * BS], BF16, kind="ExternalInput").ap()
    kd_in = nc.dram_tensor("kd_t", [128, 2 * H], F32, kind="ExternalInput").ap()
    dd_in = nc.dram_tensor("ddT_t", [H, 128, 2 * BS], BF16, kind="ExternalInput").ap()

    out_tm = nc.dram_tensor("out_tm", [8 * T, DM], F32, kind="ExternalOutput").ap()

    ws_qkv = nc.dram_tensor("ws_qkv", [MC_QKV * 128, 2048], BF16, kind="Internal").ap()
    ws_gate = nc.dram_tensor("ws_gate", [KC * 128, 2048], BF16, kind="Internal").ap()
    ws_g = nc.dram_tensor("ws_g", [MC_G * 128, 2048], BF16, kind="Internal").ap()
    ws_u = nc.dram_tensor("ws_u", [MC_G * 128, 2048], BF16, kind="Internal").ap()
    ws_out = nc.dram_tensor("ws_out", [16 * 128, 2048], BF16, kind="Internal").ap()
    ws_down = nc.dram_tensor("ws_down", [44 * 128, 2048], BF16, kind="Internal").ap()
    b_ln = nc.dram_tensor("b_ln", [T, DM], BF16, kind="Internal").ap()
    b_k = nc.dram_tensor("b_k", [DM, T], BF16, kind="Internal").ap()
    b_v = nc.dram_tensor("b_v", [DM, T], BF16, kind="Internal").ap()
    xs = nc.dram_tensor("xs", [T, DM], F32, kind="Internal").ap()

    tap_shapes = {
        "ln_fm": ([128, KC * T], BF16), "q_fm": ([128, KC * T], BF16),
        "k_fm": ([128, KC * T], BF16), "v_tm": ([128, 2 * DM], BF16),
        "k_tm": ([128, 2 * DM], BF16), "attn": ([128, KC * T], BF16),
        "y_tm": ([128, TC * DM], BF16), "gu": ([128, MC_G * T], BF16),
        "f_tm": ([128, KC * T], BF16), "rstdb": ([128, T], BF16),
        "S": ([64, H * D], F32),
    }
    taps = {nm: nc.dram_tensor("tap_" + nm, *tap_shapes[nm], kind="ExternalOutput").ap()
            for nm in debug_taps}

    from contextlib import ExitStack
    ctx = ExitStack()
    sb = lambda name, shape, dt: ctx.enter_context(nc.sbuf_tensor(name, shape, dt))

    x_sb = sb("x_sb", [128, 2 * DM], F32)
    ln_st = sb("ln_st", [128, 2 * DM], BF16)
    ln_fm = sb("ln_fm", [128, KC * T], BF16)
    qy = sb("qy", [128, KC * T], BF16)
    k_fm = sb("k_fm", [128, KC * T], BF16)
    v_st = sb("v_st", [128, 2 * T], BF16)
    k_tm = sb("k_tm", [128, 2 * DM], BF16)
    v_tm = sb("v_tm", [128, 2 * DM], BF16)
    af = sb("af", [128, KC * T], BF16)
    gu_buf = sb("gu_buf", [128, MC_G * T], BF16)
    gate_st = sb("gate_st", [128, 2 * T], BF16)
    y_ap = sb("y_ap", [128, DM], BF16)
    ring_t = sb("ring_t", [128, 3 * 2048], BF16)
    S_all = sb("S_all", [64, H * D], F32)
    sdb = sb("sdb", [128, DM], BF16)
    qd_t = sb("qd_ts", [128, KC * BS], BF16)
    kd_t = sb("kd_ts", [128, 2 * H], F32)
    dd_b = sb("dd_b", [128, 2 * 2 * BS], BF16)
    awd = sb("awd", [128, 2 * 2 * BS], BF16)
    sq_st = sb("sq_st", [128, T], BF16)
    rstdb = sb("rstdb", [128, T], BF16)
    nw_s = sb("nw_s", [128, KC], F32)
    sg_s = sb("sg_s", [128, MC_G], F32)
    su_s = sb("su_s", [128, MC_G], F32)
    qqd = sb("qqd", [64, BS], F32)
    bn6 = sb("bn6", [128, 4 * 24], F32)
    mom = sb("mom", [128, 4 * 16], F32)
    rrow = sb("rrow", [1, T], F32)
    rrow_b = sb("rrow_b", [1, T], BF16)
    ones_c = sb("ones_c", [128, 1], BF16)
    ones_r = sb("ones_r", [1, 128], BF16)
    zc = sb("zc", [1, 2], BF16)
    cst = sb("cst", [128, 4], F32)
    nc.const_aps.register(F32, EPS, cst[:, 0:1])
    nc.const_aps.register(F32, -0.5, cst[:, 1:2])
    nc.const_aps.register(F32, 1.0 / DM, cst[:, 2:3])

    assert nc.sbuf_bytes_remaining >= 0, "SBUF over budget"

    ps = [ctx.enter_context(nc.psum_tensor(f"ps{i}", [128, 512], F32))
          for i in range(8)]

    names = (["pre_st0", "pre_st1", "wp0", "wp1", "wp2", "dd0", "dd1",
              "xld0", "xld1", "lnst0", "lnst1", "ln_fmd",
              "kv_st", "vst0", "vst1", "ktm", "vtm",
              "xs0", "xs1", "out0", "out1", "misc", "tapd",
              "pe", "act", "dvec", "pool", "ast0", "ast1", "ast2"])
    SM = {n: Sem(n) for n in names}
    for s_ in SM.values():
        s_.h = nc.alloc_semaphore(s_.name)

    def scr2(ws, idx):
        return ws[idx * 128:(idx + 1) * 128, :]

    pieces = []
    for m in range(MC_QKV):
        pieces.append((scr2(ws_qkv, m), scr2(qkvT, m), None, 128))
    for m in range(KC):
        pieces.append((scr2(ws_gate, m), scr2(gateT, m), None, 128))
    for i in range(16):
        pieces.append((scr2(ws_out, i), scr2(outT, i), None, 512))
    for m in range(MC_G):
        pieces.append((scr2(ws_g, m), scr2(wqgT, m), "m128", 128))
    for m in range(MC_G):
        pieces.append((scr2(ws_u, m), scr2(wquT, m), "m128", 128))
    for i in range(44):
        fg = i // 11
        pieces.append((scr2(ws_down, i), scr2(wqdT, i), ("down", fg), 512))
    NP = len(pieces)
    P_QKV, P_GATE, P_OUT, P_G, P_U, P_DOWN = 0, 48, 64, 80, 124, 168

    base = 1.0 / 2.0 ** (8.0 / H)
    rate = base ** (np.arange(H, dtype=np.float64) + 1.0)
    factor = 1.0 - LAYER_IDX / (NUM_LAYERS - 1 + 1e-5) + 1e-5
    bd_py = np.exp(-rate * factor * BS)

    def schedule(w: Walk):
        s = SM
        EC = {"tensor": s["pe"], "scalar": s["act"], "vector": s["dvec"],
              "gpsimd": s["pool"]}

        def dma(who, out, in_, sem, **kw):
            w.emit(who, lambda e: e.dma_start(out, in_, **kw), [(sem, 16)])
            return (sem, sem.n)

        def comp(who, build, inc=False):
            w.emit(who, build, [(EC[who], 1)] if inc else [])
            return (EC[who], EC[who].n) if inc else None

        def pe(build, inc=False):
            return comp("tensor", build, inc)

        def av(build, inc=False):
            return comp("scalar", build, inc)

        def dv(build, inc=False):
            return comp("vector", build, inc)

        def wev(who, ev):
            if ev is not None:
                w.wait(who, ev[0], ev[1])

        def pe_end(bank_ap):
            return pe(lambda e: e.matmul(bank_ap[0:1, 0:128], zc[0:1, 0:1],
                                         ones_r[0:1, 0:128], start=False,
                                         stop=True), inc=True)

        def dv2(build):
            dv(build)
            return dv(lambda e: e.memset(cst[:, 3:4], 0.0), inc=True)

        def av2(build):
            av(build)
            return av(lambda e: e.activation(cst[0:1, 3:4], cst[0:1, 3:4],
                                             AF.Identity), inc=True)

        def dve_guard():
            dv(lambda e: e.memset(cst[:, 3:4], 0.0))

        bank_w = [None] * 8
        bank_r = [[] for _ in range(8)]

        def bank_pre_write(i, who="tensor"):
            for ev in bank_r[i]:
                wev(who, ev)
            bank_r[i] = []

        def bank_read(i, who, ev=None):
            wev(who, bank_w[i])

        r_tile, r_x, r_lnst = Ring(3), Ring(2), Ring(2)
        r_vst, r_gst, r_dd, r_awd = Ring(2), Ring(2), Ring(2), Ring(1)
        hbs_last = {}
        mom_free = {}
        qk_rdy = {}
        attn_evac = {}
        lnfm_free_evs = []
        qqd_free = None

        for dst, srcin in [(qd_t[:, :], qd_in), (kd_t[:, :], kd_in),
                           (nw_s[:, :], nw_in), (sg_s[:, :], sg_in),
                           (su_s[:, :], su_in)]:
            dma("gpsimd", dst, srcin, s["misc"])
        dma("gpsimd", sdb[:, :], sd_in, s["misc"])
        sdb_ld = (s["misc"], s["misc"].n)
        dv(lambda e: e.memset(cst[:, 0:1], EPS))
        dv(lambda e: e.memset(cst[:, 1:2], -0.5))
        dv(lambda e: e.memset(cst[:, 2:3], 1.0 / DM))
        dv(lambda e: e.memset(ones_c[:, :], 1.0))
        dv(lambda e: e.memset(ones_r[:, :], 1.0))
        dv(lambda e: e.memset(zc[:, :], 0.0))
        init_dve = dv(lambda e: e.memset(S_all[:, :], 0.0), inc=True)
        wev("scalar", init_dve)
        wev("tensor", init_dve)
        w.wait("tensor", s["misc"])

        pre_ev = {}
        for p_i in range(NP):
            scr, hsrc, adj, wdt = pieces[p_i]
            pre_ev[p_i] = dma("gpsimd", scr, hsrc, s["pre_st0"])

        for eng_ in ("sync", "scalar", "vector", "tensor", "gpsimd"):
            w.wait(eng_, s["pre_st0"], s["pre_st0"].n)
            w.wait(eng_, s["pre_st1"], s["pre_st1"].n)

        pc = [0]
        piece_rdy, piece_slot = {}, {}

        def load_piece(p_i):
            g = pc[0]
            slot = ring_t[:, (g % 3) * 2048:(g % 3 + 1) * 2048]
            r_tile.acquire(w, "sync", g)
            wev("sync", pre_ev[p_i])
            piece_rdy[g] = dma("sync", slot, pieces[p_i][0], s[f"wp{g % 3}"])
            piece_slot[g] = slot
            pc[0] += 1
            return g

        def use_piece(g):
            wev("tensor", piece_rdy[g])
            return piece_slot[g]

        def free_piece(g, ev):
            r_tile.release(g, ev[0], ev[1])

        def free_piece_multi(g, evs):
            r_tile.release_multi(g, evs)

        def adjust_piece(j, g, p_i):
            if j != 0:
                return None, None
            slot = piece_slot[g]
            adj = pieces[p_i][2]
            wev("vector", piece_rdy[g])
            adj_ev = None
            if adj == "m128":
                adj_ev = dv(lambda e, sl=slot: e.tensor_scalar_add(
                    sl, sl, -128.0), inc=True)
            else:
                fg = adj[1]
                wev("vector", sdb_ld)
                dv(lambda e, sl=slot: e.tensor_scalar_add(sl, sl, -128.0))
                for kb in range(4):
                    adj_ev = dv(lambda e, sl=slot, kb=kb, fg=fg:
                                e.tensor_mul(sl[:, kb * 512:(kb + 1) * 512],
                                             sl[:, kb * 512:(kb + 1) * 512],
                                             sdb[:, fg * 512:(fg + 1) * 512]),
                                inc=(kb == 3))
            wev("scalar", adj_ev)
            st_ev = dma("scalar", pieces[p_i][0], slot, s[f"ast{g % 3}"])
            pre_ev[p_i] = st_ev
            return adj_ev, st_ev

        def ln_stats(site, tc, bs_ev, reg=0):
            mo = mom[:, site * 16 + tc * 4: site * 16 + tc * 4 + 4]
            evt = mom_free.get((site, tc))
            if evt is not None:
                wev("vector", evt)
            wev("vector", bs_ev)
            aggr_ev = dv(lambda e: e.bn_aggr(
                mo[:, 0:2], bn6[:, reg * 24:(reg + 1) * 24]), inc=True)
            wev("scalar", aggr_ev)
            av(lambda e: e.activation(mo[:, 2:3], mo[:, 1:2], AF.Ln, bias=EPS))
            exp_ev = av(lambda e: e.activation(mo[:, 2:3], mo[:, 2:3], AF.Exp,
                                               scale=-0.5), inc=True)
            wev("vector", exp_ev)
            neg_ev = dv(lambda e: e.scalar_tensor_tensor(
                mo[:, 3:4], mo[:, 0:1], -1.0, mo[:, 2:3],
                ALU.mult, ALU.mult), inc=True)
            return mo[:, 2:3], mo[:, 3:4], neg_ev

        vtm_prev = ktm_prev = lnfm_prev = 0
        fap_evs_prev = {}
        xs_guard = []
        att_pe_ev = None
        for j in range(n_tiles):
            r0 = j * T

            if j == 4:
                wev("vector", att_pe_ev)
                ms_ev = dv(lambda e: e.memset(S_all[:, :], 0.0), inc=True)
                for h in range(H):
                    hbs_last[h] = ms_ev

            for tc in range(TC):
                xsl = x_sb[:, (tc % 2) * DM:(tc % 2 + 1) * DM]
                r_x.acquire(w, "scalar", tc)
                xev = dma("scalar", xsl,
                          xin[r0 + tc * 128:r0 + (tc + 1) * 128, :],
                          s[f"xld{tc % 2}"])
                wev("vector", xev)
                bs_ev = None
                for q in range(4):
                    bs_ev = dv(lambda e, q=q, xsl=xsl, tc=tc: e.bn_stats(
                        bn6[:, tc * 24 + q * 6:tc * 24 + (q + 1) * 6],
                        xsl[:, q * 512:(q + 1) * 512]), inc=(q == 3))
                sc, bi, st_ev = ln_stats(0, tc, bs_ev, reg=tc)
                lsl = ln_st[:, (tc % 2) * DM:(tc % 2 + 1) * DM]
                r_lnst.acquire(w, "scalar", tc)
                wev("scalar", st_ev)
                ap_ev = av(lambda e, lsl=lsl, xsl=xsl, sc=sc, bi=bi: e.activation(
                    lsl, xsl, AF.Identity, scale=sc, bias=bi), inc=True)
                r_x.release(tc, *ap_ev)
                mom_free[(0, tc)] = ap_ev
                wev("scalar", ap_ev)
                w.wait("scalar", s["ln_fmd"], lnfm_prev)
                lev = dma("scalar", b_ln[tc * 128:(tc + 1) * 128, :], lsl,
                          s[f"lnst{tc % 2}"])
                r_lnst.release(tc, *lev)

            w.wait("scalar", s["lnst0"])
            w.wait("scalar", s["lnst1"])
            lnfm_base = s["ln_fmd"].n
            for c in range(KC):
                if fap_evs_prev:
                    wev("scalar", fap_evs_prev[c // 4])
                dma("scalar", ln_fm[:, c * T:(c + 1) * T],
                    b_ln[:, c * 128:(c + 1) * 128], s["ln_fmd"], transpose=True)
            lnfm_val = s["ln_fmd"].n

            for mi in range(MC_QKV):
                m = [*range(32, 48), *range(16, 32), *range(0, 16)][mi]
                g = load_piece(P_QKV + m)
                bank = mi % 4
                slot = use_piece(g)
                bank_pre_write(bank)
                for kc in range(KC):
                    w.wait("tensor", s["ln_fmd"], lnfm_base + (kc + 1) * 16)
                    pe(lambda e, kc=kc, slot=slot, bank=bank: e.matmul(
                        ps[bank][:, :], slot[:, kc * 128:(kc + 1) * 128],
                        ln_fm[:, kc * T:(kc + 1) * T],
                        start=(kc == 0), stop=False))
                mm_ev = pe_end(ps[bank])
                bank_w[bank] = mm_ev
                free_piece(g, mm_ev)
                wev("scalar", mm_ev)
                if m < 16:
                    ev = av2(lambda e, m=m, bank=bank: e.activation(
                        qy[:, m * T:(m + 1) * T], ps[bank][:, :], AF.Silu))
                    bank_r[bank].append(ev)
                    qk_rdy[("q", m)] = ev
                elif m < 32:
                    ev = av2(lambda e, m=m, bank=bank: e.activation(
                        k_fm[:, (m - 16) * T:(m - 15) * T], ps[bank][:, :],
                        AF.Silu))
                    bank_r[bank].append(ev)
                    qk_rdy[("k", m - 16)] = ev
                    wev("scalar", ev)
                    w.wait("scalar", s["ktm"], ktm_prev)
                    dma("scalar", b_k[(m - 16) * 128:(m - 15) * 128, :],
                        k_fm[:, (m - 16) * T:(m - 15) * T], s["kv_st"])
                else:
                    vs = v_st[:, (m % 2) * T:(m % 2 + 1) * T]
                    r_vst.acquire(w, "scalar", m)
                    ev = av(lambda e, vs=vs, bank=bank: e.activation(
                        vs, ps[bank][:, :], AF.Silu), inc=True)
                    bank_r[bank].append(ev)
                    wev("scalar", ev)
                    w.wait("scalar", s["vtm"], vtm_prev)
                    vev = dma("scalar", b_v[(m - 32) * 128:(m - 31) * 128, :],
                              vs, s[f"vst{m % 2}"])
                    r_vst.release(m, *vev)
            kv_done = (s["kv_st"].n, s["vst0"].n, s["vst1"].n)

            if stop_after == "qkv":
                lnfm_free_evs.append(mm_ev)
                continue

            for blk in range(NB):
                wev("scalar", att_pe_ev)
                w.wait("scalar", s["kv_st"], kv_done[0])
                w.wait("scalar", s["vst0"], kv_done[1])
                w.wait("scalar", s["vst1"], kv_done[2])
                for jc in range(2):
                    dma("scalar", k_tm[:, jc * DM:(jc + 1) * DM],
                        b_k[:, blk * 256 + jc * 128:blk * 256 + (jc + 1) * 128],
                        s["ktm"], transpose=True)
                    dma("scalar", v_tm[:, jc * DM:(jc + 1) * DM],
                        b_v[:, blk * 256 + jc * 128:blk * 256 + (jc + 1) * 128],
                        s["vtm"], transpose=True)
                w.wait("vector", s["ktm"])
                wev("vector", att_pe_ev)
                for jc in range(2):
                    for h in range(H):
                        dv(lambda e, jc=jc, h=h: e.tensor_scalar_mul(
                            k_tm[:, jc * DM + h * 64:jc * DM + (h + 1) * 64],
                            k_tm[:, jc * DM + h * 64:jc * DM + (h + 1) * 64],
                            kd_t[:, jc * H + h:jc * H + h + 1]))
                kd_ev = dv(lambda e: e.memset(cst[:, 3:4], 0.0), inc=True)
                vtm_val = s["vtm"].n

                last_state_ev = None
                for h in range(H):
                    hb = (j * NB + blk) * H + h
                    ab, tb = 4 + (h % 2), 6 + (h % 2)
                    hc, hr = h // 2, (h % 2) * 64
                    qcol = hc * T + blk * 256
                    ddl = dd_b[:, (h % 2) * 512:(h % 2 + 1) * 512]
                    r_dd.acquire(w, "scalar", hb)
                    dd_ev = dma("scalar", ddl, dd_in[h], s[f"dd{h % 2}"])
                    wev("vector", qk_rdy[("q", hc)])
                    if hb > 0:
                        wev("vector", qqd_free)
                    qqd_ev = dv2(lambda e, hr=hr, qcol=qcol, hc=hc: e.tensor_mul(
                        qqd[:, :], qy[hr:hr + 64, qcol:qcol + 256],
                        qd_t[hr:hr + 64, hc * 256:(hc + 1) * 256]))
                    wev("tensor", qk_rdy[("k", hc)])
                    wev("tensor", qk_rdy[("q", hc)])
                    bank_pre_write(ab)
                    for jc in range(2):
                        pe(lambda e, jc=jc, hr=hr, qcol=qcol, ab=ab:
                           e.matmul(
                               ps[ab][:, jc * 256:(jc + 1) * 256],
                               k_fm[hr:hr + 64,
                                    qcol + jc * 128:qcol + (jc + 1) * 128],
                               qy[hr:hr + 64, qcol:qcol + 256],
                               start=(jc == 0), stop=False))
                    aw_ev = pe_end(ps[ab])
                    bank_w[ab] = aw_ev
                    asl = awd[:, 0:512]
                    r_awd.acquire(w, "vector", hb)
                    wev("vector", aw_ev)
                    wev("vector", dd_ev)
                    dve_guard()
                    awd_ev = dv2(lambda e, asl=asl, ab=ab, ddl=ddl: e.tensor_mul(
                        asl, ps[ab][:, :], ddl))
                    bank_r[ab].append(awd_ev)
                    r_dd.release(hb, *awd_ev)
                    bank_pre_write(tb)
                    wev("tensor", awd_ev)
                    w.wait("tensor", s["vtm"], vtm_val)
                    for jc in range(2):
                        pe(lambda e, jc=jc, h=h, tb=tb, asl=asl: e.matmul(
                            ps[tb][0:64, 0:256],
                            v_tm[:, jc * DM + h * 64:jc * DM + (h + 1) * 64],
                            asl[:, jc * 256:(jc + 1) * 256],
                            start=(jc == 0), stop=False))
                    wev("tensor", qqd_ev)
                    wev("tensor", hbs_last.get(h))
                    pe(lambda e, h=h, tb=tb: e.matmul(
                        ps[tb][0:64, 0:256], S_all[:, h * 64:(h + 1) * 64],
                        qqd[:, :], start=False, stop=False))
                    at_ev = pe_end(ps[tb])
                    bank_w[tb] = at_ev
                    r_awd.release(hb, *at_ev)
                    qqd_free = at_ev
                    wev("scalar", at_ev)
                    ev_ev = av(lambda e, hr=hr, qcol=qcol, tb=tb: e.activation(
                        af[hr:hr + 64, qcol:qcol + 256], ps[tb][0:64, 0:256],
                        AF.Identity), inc=True)
                    bank_r[tb].append(ev_ev)
                    attn_evac[(j, blk, h)] = ev_ev
                    bank_pre_write(tb)
                    wev("tensor", kd_ev)
                    for jc in range(2):
                        pe(lambda e, jc=jc, h=h, tb=tb: e.matmul(
                            ps[tb][0:64, 0:64],
                            k_tm[:, jc * DM + h * 64:jc * DM + (h + 1) * 64],
                            v_tm[:, jc * DM + h * 64:jc * DM + (h + 1) * 64],
                            start=(jc == 0), stop=False))
                    st_ev = pe_end(ps[tb])
                    bank_w[tb] = st_ev
                    last_state_ev = st_ev
                    wev("vector", st_ev)
                    dve_guard()
                    dv(lambda e, h=h: e.tensor_scalar_mul(
                        S_all[:, h * 64:(h + 1) * 64],
                        S_all[:, h * 64:(h + 1) * 64], float(bd_py[h])))
                    su_ev = dv2(lambda e, h=h, tb=tb: e.tensor_add(
                        S_all[:, h * 64:(h + 1) * 64],
                        S_all[:, h * 64:(h + 1) * 64], ps[tb][0:64, 0:64]))
                    bank_r[tb].append(su_ev)
                    hbs_last[h] = su_ev
                att_pe_ev = last_state_ev
            vtm_prev, ktm_prev = s["vtm"].n, s["ktm"].n

            if stop_after == "attn":
                lnfm_free_evs.append(att_pe_ev)
                continue

            sq_free = None
            for c in range(KC):
                wev("vector", attn_evac[(j, 0, 2 * c + 1)])
                wev("vector", attn_evac[(j, 1, 2 * c + 1)])
                wev("vector", sq_free)
                sq_ev = dv(lambda e, c=c: e.tensor_mul(
                    sq_st[:, :], af[:, c * T:(c + 1) * T],
                    af[:, c * T:(c + 1) * T]), inc=True)
                wev("tensor", sq_ev)
                if c == 0:
                    bank_pre_write(4)
                mm_ev = pe(lambda e, c=c: e.matmul(
                    ps[4][0:1, :], ones_c[:, :], sq_st[:, :],
                    start=(c == 0), stop=False), inc=(c < KC - 1))
                sq_free = mm_ev
            sq_free = pe_end(ps[4])
            bank_w[4] = sq_free
            wev("scalar", sq_free)
            av(lambda e: e.activation(rrow[:, :], ps[4][0:1, :], AF.Ln,
                                      scale=1.0 / DM, bias=EPS))
            rr_ev = av2(lambda e: e.activation(rrow_b[:, :], rrow[:, :],
                                               AF.Exp, scale=-0.5))
            bank_r[4].append(rr_ev)
            wev("tensor", rr_ev)
            bank_pre_write(5)
            pe(lambda e: e.matmul(ps[5][:, :], ones_r[:, :],
                                  rrow_b[:, :], start=True, stop=False))
            bc_ev = pe_end(ps[5])
            bank_w[5] = bc_ev
            wev("scalar", bc_ev)
            rstdb_ev = av(lambda e: e.activation(rstdb[:, :], ps[5][:, :],
                                                 AF.Identity), inc=True)
            bank_r[5].append(rstdb_ev)

            gated_evs = {}
            for c in range(KC):
                g = load_piece(P_GATE + c)
                bank = c % 4
                slot = use_piece(g)
                bank_pre_write(bank)
                for kc in range(KC):
                    pe(lambda e, kc=kc, slot=slot, bank=bank: e.matmul(
                        ps[bank][:, :], slot[:, kc * 128:(kc + 1) * 128],
                        ln_fm[:, kc * T:(kc + 1) * T],
                        start=(kc == 0), stop=False))
                mm_ev = pe_end(ps[bank])
                bank_w[bank] = mm_ev
                free_piece(g, mm_ev)
                gsl = gate_st[:, (c % 2) * T:(c % 2 + 1) * T]
                r_gst.acquire(w, "scalar", c)
                wev("scalar", mm_ev)
                sig_ev = av(lambda e, gsl=gsl, bank=bank: e.activation(
                    gsl, ps[bank][:, :], AF.Sigmoid), inc=True)
                bank_r[bank].append(sig_ev)
                wev("vector", rstdb_ev)
                wev("vector", sig_ev)
                dv(lambda e, c=c: e.tensor_mul(
                    af[:, c * T:(c + 1) * T], af[:, c * T:(c + 1) * T],
                    rstdb[:, :]))
                dv(lambda e, c=c: e.tensor_scalar_mul(
                    af[:, c * T:(c + 1) * T], af[:, c * T:(c + 1) * T],
                    nw_s[:, c:c + 1]))
                g_ev = dv2(lambda e, c=c, gsl=gsl: e.tensor_mul(
                    af[:, c * T:(c + 1) * T], af[:, c * T:(c + 1) * T], gsl))
                gated_evs[c] = g_ev
                r_gst.release(c, *g_ev)
            lnfm_free_evs.append(mm_ev)

            if stop_after == "gate":
                lnfm_free_evs.append(mm_ev)
                continue

            y_bs = {}
            for fg in range(4):
                for kp in range(4):
                    g = load_piece(P_OUT + fg * 4 + kp)
                    slot = use_piece(g)
                    p_ev = None
                    for kc in range(4):
                        K = kp * 4 + kc
                        wev("tensor", gated_evs[K])
                        for t in range(TC):
                            if kp == 0 and kc == 0:
                                bank_pre_write(t)
                            p_ev = pe(lambda e, K=K, t=t, kc=kc, slot=slot:
                                      e.matmul(
                                          ps[t][:, :],
                                          af[:, K * T + t * 128:
                                             K * T + (t + 1) * 128],
                                          slot[:, kc * 512:(kc + 1) * 512],
                                          start=(kp == 0 and kc == 0),
                                          stop=False),
                                      inc=(kc == 3 and t == TC - 1))
                    free_piece(g, p_ev)
                    if kp == 3:
                        grp_ev = pe_end(ps[TC - 1])
                        for t in range(TC):
                            bank_w[t] = grp_ev
                for t in range(TC):
                    wev("scalar", bank_w[t])
                    yev = av(lambda e, t=t, fg=fg: e.activation(
                        qy[:, t * DM + fg * 512:t * DM + (fg + 1) * 512],
                        ps[t][:, :], AF.Identity), inc=True)
                    bank_r[t].append(yev)
                    wev("vector", yev)
                    y_bs[t] = dv(lambda e, t=t, fg=fg: e.bn_stats(
                        bn6[:, t * 24 + fg * 6:t * 24 + (fg + 1) * 6],
                        qy[:, t * DM + fg * 512:t * DM + (fg + 1) * 512]),
                        inc=(fg == 3))

            lny = {}
            for tc in range(TC):
                lny[tc] = ln_stats(1, tc, y_bs[tc], reg=tc)
            yap_free = None
            new_xs_guard = []
            for tc in range(TC):
                xsl = x_sb[:, (tc % 2) * DM:(tc % 2 + 1) * DM]
                r_x.acquire(w, "scalar", tc)
                xev = dma("scalar", xsl,
                          xin[r0 + tc * 128:r0 + (tc + 1) * 128, :],
                          s[f"xld{tc % 2}"])
                sc, bi, st_ev = lny[tc]
                wev("scalar", st_ev)
                wev("scalar", yap_free)
                yap_ev = av(lambda e, tc=tc, sc=sc, bi=bi: e.activation(
                    y_ap[:, :], qy[:, tc * DM:(tc + 1) * DM], AF.Identity,
                    scale=sc, bias=bi), inc=True)
                mom_free[(1, tc)] = yap_ev
                wev("vector", yap_ev)
                wev("vector", xev)
                resid_ev = dv(lambda e, xsl=xsl: e.tensor_add(
                    xsl, xsl, y_ap[:, :]), inc=True)
                yap_free = resid_ev
                bs = None
                for q in range(4):
                    bs = dv(lambda e, q=q, xsl=xsl, tc=tc: e.bn_stats(
                        bn6[:, tc * 24 + q * 6:tc * 24 + (q + 1) * 6],
                        xsl[:, q * 512:(q + 1) * 512]), inc=(q == 3))
                sc2, bi2, st2_ev = ln_stats(2, tc, bs, reg=tc)
                lsl = ln_st[:, (tc % 2) * DM:(tc % 2 + 1) * DM]
                r_lnst.acquire(w, "scalar", tc)
                wev("scalar", st2_ev)
                ap2_ev = av(lambda e, lsl=lsl, xsl=xsl, sc2=sc2, bi2=bi2:
                            e.activation(lsl, xsl, AF.Identity, scale=sc2,
                                         bias=bi2), inc=True)
                mom_free[(2, tc)] = ap2_ev
                wev("scalar", ap2_ev)
                w.wait("scalar", s["ln_fmd"], lnfm_val)
                lev = dma("scalar", b_ln[tc * 128:(tc + 1) * 128, :], lsl,
                          s[f"lnst{tc % 2}"])
                r_lnst.release(tc, *lev)
                wev("scalar", resid_ev)
                for xg in xs_guard:
                    wev("scalar", xg)
                xsev = dma("scalar", xs[tc * 128:(tc + 1) * 128, :], xsl,
                           s[f"xs{tc % 2}"])
                r_x.release(tc, *xsev)

            for ev in lnfm_free_evs:
                wev("scalar", ev)
            lnfm_free_evs = []
            w.wait("scalar", s["lnst0"])
            w.wait("scalar", s["lnst1"])
            lnfm_base = s["ln_fmd"].n
            for c in range(KC):
                dma("scalar", ln_fm[:, c * T:(c + 1) * T],
                    b_ln[:, c * 128:(c + 1) * 128], s["ln_fmd"], transpose=True)
            lnfm_val = s["ln_fmd"].n
            lnfm_prev = lnfm_val

            if stop_after == "ln2":
                continue

            for m in range(MC_G):
                g = load_piece(P_G + m)
                bank = m % 4
                adj_ev, ast_ev = adjust_piece(j, g, P_G + m)
                slot = use_piece(g)
                wev("tensor", adj_ev)
                bank_pre_write(bank)
                for kc in range(KC):
                    w.wait("tensor", s["ln_fmd"], lnfm_base + (kc + 1) * 16)
                    pe(lambda e, kc=kc, slot=slot, bank=bank: e.matmul(
                        ps[bank][:, :], slot[:, kc * 128:(kc + 1) * 128],
                        ln_fm[:, kc * T:(kc + 1) * T],
                        start=(kc == 0), stop=False))
                mm_ev = pe_end(ps[bank])
                bank_w[bank] = mm_ev
                free_piece_multi(g, [mm_ev, ast_ev])
                wev("scalar", mm_ev)
                gev = av(lambda e, m=m, bank=bank: e.activation(
                    gu_buf[:, m * T:(m + 1) * T], ps[bank][:, :], AF.Silu,
                    scale=sg_s[:, m:m + 1]), inc=True)
                bank_r[bank].append(gev)
            gu_rdy = {}
            for m in range(MC_G):
                g = load_piece(P_U + m)
                bank = m % 4
                adj_ev, ast_ev = adjust_piece(j, g, P_U + m)
                slot = use_piece(g)
                wev("tensor", adj_ev)
                bank_pre_write(bank)
                for kc in range(KC):
                    pe(lambda e, kc=kc, slot=slot, bank=bank: e.matmul(
                        ps[bank][:, :], slot[:, kc * 128:(kc + 1) * 128],
                        ln_fm[:, kc * T:(kc + 1) * T],
                        start=(kc == 0), stop=False))
                mm_ev = pe_end(ps[bank])
                bank_w[bank] = mm_ev
                free_piece_multi(g, [mm_ev, ast_ev])
                wev("vector", mm_ev)
                dve_guard()
                uev = dv2(lambda e, m=m, bank=bank: e.scalar_tensor_tensor(
                    gu_buf[:, m * T:(m + 1) * T], ps[bank][:, :],
                    su_s[:, m:m + 1], gu_buf[:, m * T:(m + 1) * T],
                    ALU.mult, ALU.mult))
                bank_r[bank].append(uev)
                gu_rdy[m] = uev

            f_bs = {}
            for fg in range(4):
                for kp in range(11):
                    g = load_piece(P_DOWN + fg * 11 + kp)
                    adj_ev, ast_ev = adjust_piece(j, g, P_DOWN + fg * 11 + kp)
                    slot = use_piece(g)
                    wev("tensor", adj_ev)
                    p_ev = None
                    for kc in range(4):
                        K = kp * 4 + kc
                        wev("tensor", gu_rdy[K])
                        for t in range(TC):
                            if kp == 0 and kc == 0:
                                bank_pre_write(t)
                            p_ev = pe(lambda e, K=K, t=t, kc=kc, slot=slot:
                                      e.matmul(
                                          ps[t][:, :],
                                          gu_buf[:, K * T + t * 128:
                                                 K * T + (t + 1) * 128],
                                          slot[:, kc * 512:(kc + 1) * 512],
                                          start=(kp == 0 and kc == 0),
                                          stop=False),
                                      inc=(kc == 3 and t == TC - 1))
                    free_piece_multi(g, [p_ev, ast_ev])
                    if kp == 10:
                        grp_ev = pe_end(ps[TC - 1])
                        for t in range(TC):
                            bank_w[t] = grp_ev
                for t in range(TC):
                    wev("scalar", bank_w[t])
                    fev = av(lambda e, t=t, fg=fg: e.activation(
                        ln_fm[:, t * DM + fg * 512:t * DM + (fg + 1) * 512],
                        ps[t][:, :], AF.Identity), inc=True)
                    bank_r[t].append(fev)
                    wev("vector", fev)
                    f_bs[t] = dv(lambda e, t=t, fg=fg: e.bn_stats(
                        bn6[:, t * 24 + fg * 6:t * 24 + (fg + 1) * 6],
                        ln_fm[:, t * DM + fg * 512:t * DM + (fg + 1) * 512]),
                        inc=(fg == 3))

            lnf = {}
            for tc in range(TC):
                lnf[tc] = ln_stats(3, tc, f_bs[tc], reg=tc)
            fap_evs_prev = {}
            for tc in range(TC):
                xsl = x_sb[:, (tc % 2) * DM:(tc % 2 + 1) * DM]
                r_x.acquire(w, "scalar", tc)
                w.wait("scalar", s["xs0"], s["xs0"].n)
                w.wait("scalar", s["xs1"], s["xs1"].n)
                xev = dma("scalar", xsl, xs[tc * 128:(tc + 1) * 128, :],
                          s[f"xld{tc % 2}"])
                sc3, bi3, st3_ev = lnf[tc]
                wev("scalar", st3_ev)
                wev("scalar", yap_free)
                fap_ev = av(lambda e, tc=tc, sc3=sc3, bi3=bi3: e.activation(
                    y_ap[:, :], ln_fm[:, tc * DM:(tc + 1) * DM], AF.Identity,
                    scale=sc3, bias=bi3), inc=True)
                mom_free[(3, tc)] = fap_ev
                fap_evs_prev[tc] = fap_ev
                wev("vector", fap_ev)
                wev("vector", xev)
                fr_ev = dv(lambda e, xsl=xsl: e.tensor_add(
                    xsl, xsl, y_ap[:, :]), inc=True)
                yap_free = fr_ev
                wev("scalar", fr_ev)
                oev = dma("scalar",
                          out_tm[r0 + tc * 128:r0 + (tc + 1) * 128, :], xsl,
                          s[f"out{tc % 2}"])
                r_x.release(tc, *oev)
                new_xs_guard.append(xev)
            xs_guard = new_xs_guard

        w.wait("scalar", s["out0"], s["out0"].n)
        w.wait("scalar", s["out1"], s["out1"].n)
        w.wait("gpsimd", s["pre_st0"], s["pre_st0"].n)
        w.wait("gpsimd", s["pre_st1"], s["pre_st1"].n)

        tap_src = {"ln_fm": ln_fm, "q_fm": qy, "k_fm": k_fm, "v_tm": v_tm,
                   "k_tm": k_tm, "attn": af, "y_tm": qy, "gu": gu_buf,
                   "f_tm": ln_fm, "rstdb": rstdb, "S": S_all}
        if taps:
            for nm_ in ["ln_fmd", "kv_st", "vst0", "vst1", "ktm", "vtm",
                        "misc", "pe", "act", "dvec", "pool"]:
                w.wait("gpsimd", s[nm_], s[nm_].n)
        for nm, ap in taps.items():
            src = tap_src[nm]
            dma("gpsimd", ap, src[:, 0:ap.shape[-1]], s["tapd"])
        w.wait("gpsimd", s["tapd"], s["tapd"].n)

    with nc.Block() as block:
        for name in ["sync", "scalar", "vector", "tensor", "gpsimd"]:
            def make(nm):
                def f(eng):
                    schedule(Walk(nm, eng, list(SM.values())))
                return f
            getattr(block, name)(make(name))

    ctx.close()
    return nc



def _tables():
    base = 1.0 / 2.0 ** (8.0 / H)
    rate = base ** (np.arange(H, dtype=np.float64) + 1.0)
    factor = 1.0 - LAYER_IDX / (NUM_LAYERS - 1 + 1e-5) + 1e-5
    slope = (rate * factor)[:, None]
    i = np.arange(BS, dtype=np.float64)[None, :]
    qd = np.exp(-slope * (i + 1.0))
    kd = np.exp(-slope * (BS - (i + 1.0)))
    dif = i[0][:, None] - i[0][None, :]
    ddm = np.exp(-slope[:, None] * np.maximum(dif, 0.0)) * (dif >= 0)
    qd_t = np.zeros((128, KC * BS), np.float64)
    for c in range(KC):
        for half in range(2):
            qd_t[half * 64:(half + 1) * 64, c * BS:(c + 1) * BS] = qd[2 * c + half]
    kd_t = np.zeros((128, 2 * H), np.float64)
    for jc in range(2):
        for h in range(H):
            kd_t[:, jc * H + h] = kd[h, jc * 128:(jc + 1) * 128]
    ddT = np.zeros((H, 128, 2 * BS), np.float64)
    for jc in range(2):
        ddT[:, :, jc * BS:(jc + 1) * BS] = np.transpose(
            ddm[:, :, jc * 128:(jc + 1) * 128], (0, 2, 1))
    return (np.ascontiguousarray(qd_t.astype(ml_dtypes.bfloat16)),
            np.ascontiguousarray(kd_t.astype(np.float32)),
            np.ascontiguousarray(ddT.astype(ml_dtypes.bfloat16)))


def _prep_shared(inp):
    qkv_w = np.asarray(inp["L0_attn_qkv_w"])
    idx = np.arange(3 * DM)
    part, rem = idx // DM, idx % DM
    perm = (rem // D) * (3 * D) + part * D + (rem % D)
    qkvT = qkv_w[perm].T.astype(np.float32)

    def mtile(wT):
        K_, M_ = wT.shape
        return np.ascontiguousarray(
            wT.reshape(K_ // 128, 128, M_ // 128, 128)
            .transpose(2, 1, 0, 3).reshape(M_ // 128 * 128, K_ // 128 * 128)
            .reshape(M_ // 128 * 128, K_))

    def ftile(wT):
        K_, M_ = wT.shape
        a = wT.reshape(K_ // 512, 4, 128, M_ // 512, 512)
        a = a.transpose(3, 0, 2, 1, 4)
        return np.ascontiguousarray(a.reshape(M_ // 512 * (K_ // 512) * 128,
                                              2048))

    qd_t, kd_t, ddT = _tables()
    return dict(
        qkvT=mtile(qkvT),
        gateT=mtile(np.asarray(inp["L0_attn_gate_w"]).T.astype(np.float32)),
        outT=ftile(np.asarray(inp["L0_attn_out_w"]).T.astype(np.float32)),
        wqgT=mtile(np.asarray(inp["L0_ffn_wq_gate"]).T.astype(np.int32)),
        wquT=mtile(np.asarray(inp["L0_ffn_wq_up"]).T.astype(np.int32)),
        wqdT=ftile(np.asarray(inp["L0_ffn_wq_down"]).T.astype(np.int32)),
        sg_p=np.ascontiguousarray(
            np.asarray(inp["L0_ffn_s_gate"]).astype(np.float32).reshape(MC_G, 128).T),
        su_p=np.ascontiguousarray(
            np.asarray(inp["L0_ffn_s_up"]).astype(np.float32).reshape(MC_G, 128).T),
        sd_bc=np.ascontiguousarray(np.broadcast_to(
            np.asarray(inp["L0_ffn_s_down"]).astype(np.float32)[None, :], (128, DM))),
        nw_p=np.ascontiguousarray(
            np.asarray(inp["L0_attn_norm_w"]).astype(np.float32).reshape(KC, 128).T),
        qd_t=qd_t, kd_t=kd_t, ddT_t=ddT)


def kernel(**inputs):
    x = np.asarray(inputs["L0_attn_hidden_states"])
    shared = _prep_shared(inputs)
    nc = build_program(n_tiles=8)
    in_maps = []
    for c in range(NCORE):
        m = dict(shared)
        m["xin"] = np.ascontiguousarray(
            x[2 * c:2 * c + 2].reshape(8 * T, DM).astype(np.float32))
        in_maps.append(m)
    res = run_bass_kernel_spmd(nc, in_maps, core_ids=list(range(NCORE)))
    outs = [r["out_tm"].reshape(2, 2048, DM) for r in res.results]
    return np.concatenate(outs, axis=0)


# version 10 (final)
# speedup vs baseline: 1.0469x; 1.0469x over previous
import sys

sys.path.insert(0, "/opt/trn_rl_repo")

import numpy as np
import ml_dtypes

import concourse.bass as bass
import concourse.mybir as mybir
from concourse.bass_utils import run_bass_kernel_spmd
import concourse.bass_utils as _bass_utils

_orig_run_command = _bass_utils.run_command


def _patched_run_command(cmd, **kw):
    if isinstance(cmd, list):
        cmd = [("--enable-ldw-opt=true" if c == "--enable-ldw-opt=false" else c)
               for c in cmd]
    return _orig_run_command(cmd, **kw)


_bass_utils.run_command = _patched_run_command

F32 = mybir.dt.float32
BF16 = mybir.dt.bfloat16
I32 = mybir.dt.int32
AF = mybir.ActivationFunctionType
ALU = mybir.AluOpType

H, D, BS = 32, 64, 256
DM = 2048
HID = 5632
T = 512
NB = 2
EPS = 1e-5
KC = DM // 128
MC_QKV = 3 * DM // 128
MC_G = HID // 128
TC = T // 128
NCORE = 8
LAYER_IDX, NUM_LAYERS = 0, 32


class Sem:
    __slots__ = ("name", "h", "n")

    def __init__(self, name):
        self.name, self.h, self.n = name, None, 0


class Walk:

    def __init__(self, who, eng, sems):
        self.who = who
        self.eng = eng
        for s in sems:
            s.n = 0
        self.last_wait = {}

    def on(self, name):
        return self.who == name

    def wait(self, who, sem, val=None):
        v = sem.n if val is None else val
        if self.who == who and v > 0:
            if v > self.last_wait.get(sem.name, 0):
                self.eng.wait_ge(sem.h, v)
                self.last_wait[sem.name] = v

    def emit(self, who, build, incs=()):
        ins = build(self.eng) if self.who == who else None
        for i, (sem, k) in enumerate(incs):
            sem.n += k
            if ins is not None:
                if i == 0:
                    ins.then_inc(sem.h, k)
                else:
                    self.eng.nop().then_inc(sem.h, k)
        return ins

    def mark(self, who, sem, k=1):
        sem.n += k
        if self.who == who:
            self.eng.nop().then_inc(sem.h, k)


class Ring:
    def __init__(self, nslots):
        self.n = nslots
        self.free = [None] * nslots

    def acquire(self, w, who, idx):
        evt = self.free[idx % self.n]
        if evt is not None:
            for sem, val in evt:
                w.wait(who, sem, val)

    def release(self, idx, sem, val=None):
        self.free[idx % self.n] = [(sem, sem.n if val is None else val)]

    def release_multi(self, idx, evs):
        self.free[idx % self.n] = [e for e in evs if e is not None]


def build_program(n_tiles=8, stop_after=None, debug_taps=()):
    nc = bass.Bass("TRN2", target_bir_lowering=False)

    xin = nc.dram_tensor("xin", [8 * T, DM], F32, kind="ExternalInput").ap()
    qkvT = nc.dram_tensor("qkvT", [MC_QKV * 128, 2048], F32, kind="ExternalInput").ap()
    gateT = nc.dram_tensor("gateT", [KC * 128, 2048], F32, kind="ExternalInput").ap()
    outT = nc.dram_tensor("outT", [16 * 128, 2048], F32, kind="ExternalInput").ap()
    wqgT = nc.dram_tensor("wqgT", [MC_G * 128, 2048], I32, kind="ExternalInput").ap()
    wquT = nc.dram_tensor("wquT", [MC_G * 128, 2048], I32, kind="ExternalInput").ap()
    wqdT = nc.dram_tensor("wqdT", [44 * 128, 2048], I32, kind="ExternalInput").ap()
    sg_in = nc.dram_tensor("sg_p", [128, MC_G], F32, kind="ExternalInput").ap()
    su_in = nc.dram_tensor("su_p", [128, MC_G], F32, kind="ExternalInput").ap()
    sd_in = nc.dram_tensor("sd_bc", [128, DM], F32, kind="ExternalInput").ap()
    nw_in = nc.dram_tensor("nw_p", [128, KC], F32, kind="ExternalInput").ap()
    qd_in = nc.dram_tensor("qd_t", [128, KC * BS], BF16, kind="ExternalInput").ap()
    kd_in = nc.dram_tensor("kd_t", [128, 2 * H], F32, kind="ExternalInput").ap()
    dd_in = nc.dram_tensor("ddT_t", [H, 128, 2 * BS], BF16, kind="ExternalInput").ap()

    out_tm = nc.dram_tensor("out_tm", [8 * T, DM], F32, kind="ExternalOutput").ap()

    ws_qkv = nc.dram_tensor("ws_qkv", [MC_QKV * 128, 2048], BF16, kind="Internal").ap()
    ws_gate = nc.dram_tensor("ws_gate", [KC * 128, 2048], BF16, kind="Internal").ap()
    ws_g = nc.dram_tensor("ws_g", [MC_G * 128, 2048], BF16, kind="Internal").ap()
    ws_u = nc.dram_tensor("ws_u", [MC_G * 128, 2048], BF16, kind="Internal").ap()
    ws_out = nc.dram_tensor("ws_out", [16 * 128, 2048], BF16, kind="Internal").ap()
    ws_down = nc.dram_tensor("ws_down", [44 * 128, 2048], BF16, kind="Internal").ap()
    b_ln = nc.dram_tensor("b_ln", [T, DM], BF16, kind="Internal").ap()
    b_k = nc.dram_tensor("b_k", [DM, T], BF16, kind="Internal").ap()
    b_v = nc.dram_tensor("b_v", [DM, T], BF16, kind="Internal").ap()
    xs = nc.dram_tensor("xs", [T, DM], F32, kind="Internal").ap()

    tap_shapes = {
        "ln_fm": ([128, KC * T], BF16), "q_fm": ([128, KC * T], BF16),
        "k_fm": ([128, KC * T], BF16), "v_tm": ([128, 2 * DM], BF16),
        "k_tm": ([128, 2 * DM], BF16), "attn": ([128, KC * T], BF16),
        "y_tm": ([128, TC * DM], BF16), "gu": ([128, MC_G * T], BF16),
        "f_tm": ([128, KC * T], BF16), "rstdb": ([128, T], BF16),
        "S": ([64, H * D], F32),
    }
    taps = {nm: nc.dram_tensor("tap_" + nm, *tap_shapes[nm], kind="ExternalOutput").ap()
            for nm in debug_taps}

    from contextlib import ExitStack
    ctx = ExitStack()
    sb = lambda name, shape, dt: ctx.enter_context(nc.sbuf_tensor(name, shape, dt))

    x_sb = sb("x_sb", [128, 2 * DM], F32)
    ln_st = sb("ln_st", [128, 2 * DM], BF16)
    ln_fm = sb("ln_fm", [128, KC * T], BF16)
    qy = sb("qy", [128, KC * T], BF16)
    k_fm = sb("k_fm", [128, KC * T], BF16)
    v_st = sb("v_st", [128, 2 * T], BF16)
    k_tm = sb("k_tm", [128, 2 * DM], BF16)
    v_tm = sb("v_tm", [128, 2 * DM], BF16)
    af = sb("af", [128, KC * T], BF16)
    gu_buf = sb("gu_buf", [128, MC_G * T], BF16)
    gate_st = sb("gate_st", [128, 2 * T], BF16)
    y_ap = sb("y_ap", [128, DM], BF16)
    ring_t = sb("ring_t", [128, 3 * 2048], BF16)
    S_all = sb("S_all", [64, H * D], F32)
    sdb = sb("sdb", [128, DM], BF16)
    qd_t = sb("qd_ts", [128, KC * BS], BF16)
    kd_t = sb("kd_ts", [128, 2 * H], F32)
    dd_b = sb("dd_b", [128, 2 * 2 * BS], BF16)
    awd = sb("awd", [128, 2 * 2 * BS], BF16)
    sq_st = sb("sq_st", [128, T], BF16)
    rstdb = sb("rstdb", [128, T], BF16)
    nw_s = sb("nw_s", [128, KC], F32)
    sg_s = sb("sg_s", [128, MC_G], F32)
    su_s = sb("su_s", [128, MC_G], F32)
    qqd = sb("qqd", [64, BS], F32)
    bn6 = sb("bn6", [128, 4 * 24], F32)
    mom = sb("mom", [128, 4 * 16], F32)
    rrow = sb("rrow", [1, T], F32)
    rrow_b = sb("rrow_b", [1, T], BF16)
    ones_c = sb("ones_c", [128, 1], BF16)
    ones_r = sb("ones_r", [1, 128], BF16)
    zc = sb("zc", [1, 2], BF16)
    cst = sb("cst", [128, 4], F32)
    nc.const_aps.register(F32, EPS, cst[:, 0:1])
    nc.const_aps.register(F32, -0.5, cst[:, 1:2])
    nc.const_aps.register(F32, 1.0 / DM, cst[:, 2:3])

    assert nc.sbuf_bytes_remaining >= 0, "SBUF over budget"

    ps = [ctx.enter_context(nc.psum_tensor(f"ps{i}", [128, 512], F32))
          for i in range(8)]

    names = (["pre_st0", "pre_st1", "wp0", "wp1", "wp2", "dd0", "dd1",
              "xld0", "xld1", "lnst0", "lnst1", "ln_fmd",
              "kv_st", "vst0", "vst1", "ktm", "vtm",
              "xs0", "xs1", "out0", "out1", "misc", "tapd",
              "pe", "act", "dvec", "pool", "ast0", "ast1", "ast2"])
    SM = {n: Sem(n) for n in names}
    for s_ in SM.values():
        s_.h = nc.alloc_semaphore(s_.name)

    def scr2(ws, idx):
        return ws[idx * 128:(idx + 1) * 128, :]

    pieces = []
    for m in range(MC_QKV):
        pieces.append((scr2(ws_qkv, m), scr2(qkvT, m), None, 128))
    for m in range(KC):
        pieces.append((scr2(ws_gate, m), scr2(gateT, m), None, 128))
    for i in range(16):
        pieces.append((scr2(ws_out, i), scr2(outT, i), None, 512))
    for m in range(MC_G):
        pieces.append((scr2(ws_g, m), scr2(wqgT, m), "m128", 128))
    for m in range(MC_G):
        pieces.append((scr2(ws_u, m), scr2(wquT, m), "m128", 128))
    for i in range(44):
        fg = i // 11
        pieces.append((scr2(ws_down, i), scr2(wqdT, i), ("down", fg), 512))
    NP = len(pieces)
    P_QKV, P_GATE, P_OUT, P_G, P_U, P_DOWN = 0, 48, 64, 80, 124, 168

    base = 1.0 / 2.0 ** (8.0 / H)
    rate = base ** (np.arange(H, dtype=np.float64) + 1.0)
    factor = 1.0 - LAYER_IDX / (NUM_LAYERS - 1 + 1e-5) + 1e-5
    bd_py = np.exp(-rate * factor * BS)

    def schedule(w: Walk):
        s = SM
        EC = {"tensor": s["pe"], "scalar": s["act"], "vector": s["dvec"],
              "gpsimd": s["pool"]}

        def dma(who, out, in_, sem, **kw):
            w.emit(who, lambda e: e.dma_start(out, in_, **kw), [(sem, 16)])
            return (sem, sem.n)

        def comp(who, build, inc=False):
            w.emit(who, build, [(EC[who], 1)] if inc else [])
            return (EC[who], EC[who].n) if inc else None

        pend_end = [None]

        def pe(build, inc=False):
            if pend_end[0] is not None:
                pend_end[0] = None
                inc = True
            return comp("tensor", build, inc)

        def av(build, inc=False):
            return comp("scalar", build, inc)

        def dv(build, inc=False):
            return comp("vector", build, inc)

        def wev(who, ev):
            if ev is not None:
                w.wait(who, ev[0], ev[1])

        def pe_end(bank_ap):
            assert pend_end[0] is None
            return pe(lambda e: e.matmul(bank_ap[0:1, 0:128], zc[0:1, 0:1],
                                         ones_r[0:1, 0:128], start=False,
                                         stop=True), inc=True)

        def pe_end_defer():
            assert pend_end[0] is None
            pend_end[0] = True
            return (EC["tensor"], EC["tensor"].n + 1)

        def dv2(build):
            dv(build)
            return dv(lambda e: e.memset(cst[:, 3:4], 0.0), inc=True)

        def av2(build):
            av(build)
            return av(lambda e: e.activation(cst[0:1, 3:4], cst[0:1, 3:4],
                                             AF.Identity), inc=True)

        def dve_guard():
            dv(lambda e: e.memset(cst[:, 3:4], 0.0))

        bank_w = [None] * 8
        bank_r = [[] for _ in range(8)]

        def bank_pre_write(i, who="tensor"):
            for ev in bank_r[i]:
                wev(who, ev)
            bank_r[i] = []

        def bank_read(i, who, ev=None):
            wev(who, bank_w[i])

        r_tile, r_x, r_lnst = Ring(3), Ring(2), Ring(2)
        r_vst, r_gst, r_dd, r_awd = Ring(2), Ring(2), Ring(2), Ring(1)
        hbs_last = {}
        mom_free = {}
        qk_rdy = {}
        attn_evac = {}
        lnfm_free_evs = []
        qqd_free = None

        for dst, srcin in [(qd_t[:, :], qd_in), (kd_t[:, :], kd_in),
                           (nw_s[:, :], nw_in), (sg_s[:, :], sg_in),
                           (su_s[:, :], su_in)]:
            dma("gpsimd", dst, srcin, s["misc"])
        dma("gpsimd", sdb[:, :], sd_in, s["misc"])
        sdb_ld = (s["misc"], s["misc"].n)
        dv(lambda e: e.memset(cst[:, 0:1], EPS))
        dv(lambda e: e.memset(cst[:, 1:2], -0.5))
        dv(lambda e: e.memset(cst[:, 2:3], 1.0 / DM))
        dv(lambda e: e.memset(ones_c[:, :], 1.0))
        dv(lambda e: e.memset(ones_r[:, :], 1.0))
        dv(lambda e: e.memset(zc[:, :], 0.0))
        init_dve = dv(lambda e: e.memset(S_all[:, :], 0.0), inc=True)
        wev("scalar", init_dve)
        wev("tensor", init_dve)
        w.wait("tensor", s["misc"])

        pre_ev = {}
        for p_i in range(NP):
            scr, hsrc, adj, wdt = pieces[p_i]
            pre_ev[p_i] = dma("gpsimd", scr, hsrc, s["pre_st0"])

        for eng_ in ("sync", "scalar", "vector", "tensor", "gpsimd"):
            w.wait(eng_, s["pre_st0"], s["pre_st0"].n)
            w.wait(eng_, s["pre_st1"], s["pre_st1"].n)

        pc = [0]
        piece_rdy, piece_slot = {}, {}

        def load_piece(p_i):
            g = pc[0]
            slot = ring_t[:, (g % 3) * 2048:(g % 3 + 1) * 2048]
            r_tile.acquire(w, "sync", g)
            wev("sync", pre_ev[p_i])
            piece_rdy[g] = dma("sync", slot, pieces[p_i][0], s[f"wp{g % 3}"])
            piece_slot[g] = slot
            pc[0] += 1
            return g

        def use_piece(g):
            wev("tensor", piece_rdy[g])
            return piece_slot[g]

        def free_piece(g, ev):
            r_tile.release(g, ev[0], ev[1])

        def free_piece_multi(g, evs):
            r_tile.release_multi(g, evs)

        def adjust_piece(j, g, p_i):
            if j != 0:
                return None, None
            slot = piece_slot[g]
            adj = pieces[p_i][2]
            wev("vector", piece_rdy[g])
            adj_ev = None
            if adj == "m128":
                adj_ev = dv(lambda e, sl=slot: e.tensor_scalar_add(
                    sl, sl, -128.0), inc=True)
            else:
                fg = adj[1]
                wev("vector", sdb_ld)
                dv(lambda e, sl=slot: e.tensor_scalar_add(sl, sl, -128.0))
                for kb in range(4):
                    adj_ev = dv(lambda e, sl=slot, kb=kb, fg=fg:
                                e.tensor_mul(sl[:, kb * 512:(kb + 1) * 512],
                                             sl[:, kb * 512:(kb + 1) * 512],
                                             sdb[:, fg * 512:(fg + 1) * 512]),
                                inc=(kb == 3))
            wev("scalar", adj_ev)
            st_ev = dma("scalar", pieces[p_i][0], slot, s[f"ast{g % 3}"])
            pre_ev[p_i] = st_ev
            return adj_ev, st_ev

        def ln_stats(site, tc, bs_ev, reg=0):
            mo = mom[:, site * 16 + tc * 4: site * 16 + tc * 4 + 4]
            evt = mom_free.get((site, tc))
            if evt is not None:
                wev("vector", evt)
            wev("vector", bs_ev)
            aggr_ev = dv(lambda e: e.bn_aggr(
                mo[:, 0:2], bn6[:, reg * 24:(reg + 1) * 24]), inc=True)
            wev("scalar", aggr_ev)
            av(lambda e: e.activation(mo[:, 2:3], mo[:, 1:2], AF.Ln, bias=EPS))
            exp_ev = av(lambda e: e.activation(mo[:, 2:3], mo[:, 2:3], AF.Exp,
                                               scale=-0.5), inc=True)
            wev("vector", exp_ev)
            neg_ev = dv(lambda e: e.scalar_tensor_tensor(
                mo[:, 3:4], mo[:, 0:1], -1.0, mo[:, 2:3],
                ALU.mult, ALU.mult), inc=True)
            return mo[:, 2:3], mo[:, 3:4], neg_ev

        vtm_prev = ktm_prev = lnfm_prev = 0
        fap_evs_prev = {}
        xs_guard = []
        att_pe_ev = None
        for j in range(n_tiles):
            r0 = j * T

            if j == 4:
                wev("vector", att_pe_ev)
                ms_ev = dv(lambda e: e.memset(S_all[:, :], 0.0), inc=True)
                for h in range(H):
                    hbs_last[h] = ms_ev

            for tc in range(TC):
                xsl = x_sb[:, (tc % 2) * DM:(tc % 2 + 1) * DM]
                r_x.acquire(w, "scalar", tc)
                xev = dma("scalar", xsl,
                          xin[r0 + tc * 128:r0 + (tc + 1) * 128, :],
                          s[f"xld{tc % 2}"])
                wev("vector", xev)
                bs_ev = None
                for q in range(4):
                    bs_ev = dv(lambda e, q=q, xsl=xsl, tc=tc: e.bn_stats(
                        bn6[:, tc * 24 + q * 6:tc * 24 + (q + 1) * 6],
                        xsl[:, q * 512:(q + 1) * 512]), inc=(q == 3))
                sc, bi, st_ev = ln_stats(0, tc, bs_ev, reg=tc)
                lsl = ln_st[:, (tc % 2) * DM:(tc % 2 + 1) * DM]
                r_lnst.acquire(w, "scalar", tc)
                wev("scalar", st_ev)
                ap_ev = av(lambda e, lsl=lsl, xsl=xsl, sc=sc, bi=bi: e.activation(
                    lsl, xsl, AF.Identity, scale=sc, bias=bi), inc=True)
                r_x.release(tc, *ap_ev)
                mom_free[(0, tc)] = ap_ev
                wev("scalar", ap_ev)
                w.wait("scalar", s["ln_fmd"], lnfm_prev)
                lev = dma("scalar", b_ln[tc * 128:(tc + 1) * 128, :], lsl,
                          s[f"lnst{tc % 2}"])
                r_lnst.release(tc, *lev)

            w.wait("scalar", s["lnst0"])
            w.wait("scalar", s["lnst1"])
            lnfm_base = s["ln_fmd"].n
            for c in range(KC):
                if fap_evs_prev:
                    wev("scalar", fap_evs_prev[c // 4])
                dma("scalar", ln_fm[:, c * T:(c + 1) * T],
                    b_ln[:, c * 128:(c + 1) * 128], s["ln_fmd"], transpose=True)
            lnfm_val = s["ln_fmd"].n

            for mi in range(MC_QKV):
                m = [*range(32, 48), *range(16, 32), *range(0, 16)][mi]
                g = load_piece(P_QKV + m)
                bank = mi % 4
                slot = use_piece(g)
                bank_pre_write(bank)
                last = (mi == MC_QKV - 1)
                for kc in range(KC):
                    w.wait("tensor", s["ln_fmd"], lnfm_base + (kc + 1) * 16)
                    pe(lambda e, kc=kc, slot=slot, bank=bank, last=last:
                       e.matmul(
                           ps[bank][:, :], slot[:, kc * 128:(kc + 1) * 128],
                           ln_fm[:, kc * T:(kc + 1) * T],
                           start=(kc == 0),
                           stop=(not last and kc == KC - 1)))
                mm_ev = pe_end(ps[bank]) if last else pe_end_defer()
                bank_w[bank] = mm_ev
                free_piece(g, mm_ev)
                wev("scalar", mm_ev)
                if m < 16:
                    ev = av2(lambda e, m=m, bank=bank: e.activation(
                        qy[:, m * T:(m + 1) * T], ps[bank][:, :], AF.Silu))
                    bank_r[bank].append(ev)
                    qk_rdy[("q", m)] = ev
                elif m < 32:
                    ev = av2(lambda e, m=m, bank=bank: e.activation(
                        k_fm[:, (m - 16) * T:(m - 15) * T], ps[bank][:, :],
                        AF.Silu))
                    bank_r[bank].append(ev)
                    qk_rdy[("k", m - 16)] = ev
                    wev("scalar", ev)
                    w.wait("scalar", s["ktm"], ktm_prev)
                    dma("scalar", b_k[(m - 16) * 128:(m - 15) * 128, :],
                        k_fm[:, (m - 16) * T:(m - 15) * T], s["kv_st"])
                else:
                    vs = v_st[:, (m % 2) * T:(m % 2 + 1) * T]
                    r_vst.acquire(w, "scalar", m)
                    ev = av(lambda e, vs=vs, bank=bank: e.activation(
                        vs, ps[bank][:, :], AF.Silu), inc=True)
                    bank_r[bank].append(ev)
                    wev("scalar", ev)
                    w.wait("scalar", s["vtm"], vtm_prev)
                    vev = dma("scalar", b_v[(m - 32) * 128:(m - 31) * 128, :],
                              vs, s[f"vst{m % 2}"])
                    r_vst.release(m, *vev)
            kv_done = (s["kv_st"].n, s["vst0"].n, s["vst1"].n)

            if stop_after == "qkv":
                lnfm_free_evs.append(mm_ev)
                continue

            for blk in range(NB):
                wev("scalar", att_pe_ev)
                w.wait("scalar", s["kv_st"], kv_done[0])
                w.wait("scalar", s["vst0"], kv_done[1])
                w.wait("scalar", s["vst1"], kv_done[2])
                for jc in range(2):
                    dma("scalar", k_tm[:, jc * DM:(jc + 1) * DM],
                        b_k[:, blk * 256 + jc * 128:blk * 256 + (jc + 1) * 128],
                        s["ktm"], transpose=True)
                    dma("scalar", v_tm[:, jc * DM:(jc + 1) * DM],
                        b_v[:, blk * 256 + jc * 128:blk * 256 + (jc + 1) * 128],
                        s["vtm"], transpose=True)
                w.wait("vector", s["ktm"])
                wev("vector", att_pe_ev)
                for jc in range(2):
                    for h in range(H):
                        dv(lambda e, jc=jc, h=h: e.tensor_scalar_mul(
                            k_tm[:, jc * DM + h * 64:jc * DM + (h + 1) * 64],
                            k_tm[:, jc * DM + h * 64:jc * DM + (h + 1) * 64],
                            kd_t[:, jc * H + h:jc * H + h + 1]))
                kd_ev = dv(lambda e: e.memset(cst[:, 3:4], 0.0), inc=True)
                vtm_val = s["vtm"].n

                last_state_ev = None
                for h in range(H):
                    hb = (j * NB + blk) * H + h
                    ab, tb = 4 + (h % 2), 6 + (h % 2)
                    hc, hr = h // 2, (h % 2) * 64
                    qcol = hc * T + blk * 256
                    ddl = dd_b[:, (h % 2) * 512:(h % 2 + 1) * 512]
                    r_dd.acquire(w, "scalar", hb)
                    dd_ev = dma("scalar", ddl, dd_in[h], s[f"dd{h % 2}"])
                    wev("vector", qk_rdy[("q", hc)])
                    if hb > 0:
                        wev("vector", qqd_free)
                    qqd_ev = dv2(lambda e, hr=hr, qcol=qcol, hc=hc: e.tensor_mul(
                        qqd[:, :], qy[hr:hr + 64, qcol:qcol + 256],
                        qd_t[hr:hr + 64, hc * 256:(hc + 1) * 256]))
                    wev("tensor", qk_rdy[("k", hc)])
                    wev("tensor", qk_rdy[("q", hc)])
                    bank_pre_write(ab)
                    for jc in range(2):
                        pe(lambda e, jc=jc, hr=hr, qcol=qcol, ab=ab:
                           e.matmul(
                               ps[ab][:, jc * 256:(jc + 1) * 256],
                               k_fm[hr:hr + 64,
                                    qcol + jc * 128:qcol + (jc + 1) * 128],
                               qy[hr:hr + 64, qcol:qcol + 256],
                               start=(jc == 0), stop=False))
                    aw_ev = pe_end(ps[ab])
                    bank_w[ab] = aw_ev
                    asl = awd[:, 0:512]
                    r_awd.acquire(w, "vector", hb)
                    wev("vector", aw_ev)
                    wev("vector", dd_ev)
                    dve_guard()
                    awd_ev = dv2(lambda e, asl=asl, ab=ab, ddl=ddl: e.tensor_mul(
                        asl, ps[ab][:, :], ddl))
                    bank_r[ab].append(awd_ev)
                    r_dd.release(hb, *awd_ev)
                    bank_pre_write(tb)
                    wev("tensor", awd_ev)
                    w.wait("tensor", s["vtm"], vtm_val)
                    for jc in range(2):
                        pe(lambda e, jc=jc, h=h, tb=tb, asl=asl: e.matmul(
                            ps[tb][0:64, 0:256],
                            v_tm[:, jc * DM + h * 64:jc * DM + (h + 1) * 64],
                            asl[:, jc * 256:(jc + 1) * 256],
                            start=(jc == 0), stop=False))
                    wev("tensor", qqd_ev)
                    wev("tensor", hbs_last.get(h))
                    pe(lambda e, h=h, tb=tb: e.matmul(
                        ps[tb][0:64, 0:256], S_all[:, h * 64:(h + 1) * 64],
                        qqd[:, :], start=False, stop=False))
                    at_ev = pe_end(ps[tb])
                    bank_w[tb] = at_ev
                    r_awd.release(hb, *at_ev)
                    qqd_free = at_ev
                    wev("scalar", at_ev)
                    ev_ev = av(lambda e, hr=hr, qcol=qcol, tb=tb: e.activation(
                        af[hr:hr + 64, qcol:qcol + 256], ps[tb][0:64, 0:256],
                        AF.Identity), inc=True)
                    bank_r[tb].append(ev_ev)
                    attn_evac[(j, blk, h)] = ev_ev
                    bank_pre_write(tb)
                    wev("tensor", kd_ev)
                    for jc in range(2):
                        pe(lambda e, jc=jc, h=h, tb=tb: e.matmul(
                            ps[tb][0:64, 0:64],
                            k_tm[:, jc * DM + h * 64:jc * DM + (h + 1) * 64],
                            v_tm[:, jc * DM + h * 64:jc * DM + (h + 1) * 64],
                            start=(jc == 0), stop=False))
                    st_ev = pe_end(ps[tb])
                    bank_w[tb] = st_ev
                    last_state_ev = st_ev
                    wev("vector", st_ev)
                    dve_guard()
                    dv(lambda e, h=h: e.tensor_scalar_mul(
                        S_all[:, h * 64:(h + 1) * 64],
                        S_all[:, h * 64:(h + 1) * 64], float(bd_py[h])))
                    su_ev = dv2(lambda e, h=h, tb=tb: e.tensor_add(
                        S_all[:, h * 64:(h + 1) * 64],
                        S_all[:, h * 64:(h + 1) * 64], ps[tb][0:64, 0:64]))
                    bank_r[tb].append(su_ev)
                    hbs_last[h] = su_ev
                att_pe_ev = last_state_ev
            vtm_prev, ktm_prev = s["vtm"].n, s["ktm"].n

            if stop_after == "attn":
                lnfm_free_evs.append(att_pe_ev)
                continue

            sq_free = None
            for c in range(KC):
                wev("vector", attn_evac[(j, 0, 2 * c + 1)])
                wev("vector", attn_evac[(j, 1, 2 * c + 1)])
                wev("vector", sq_free)
                sq_ev = dv(lambda e, c=c: e.tensor_mul(
                    sq_st[:, :], af[:, c * T:(c + 1) * T],
                    af[:, c * T:(c + 1) * T]), inc=True)
                wev("tensor", sq_ev)
                if c == 0:
                    bank_pre_write(4)
                mm_ev = pe(lambda e, c=c: e.matmul(
                    ps[4][0:1, :], ones_c[:, :], sq_st[:, :],
                    start=(c == 0), stop=False), inc=(c < KC - 1))
                sq_free = mm_ev
            sq_free = pe_end(ps[4])
            bank_w[4] = sq_free
            wev("scalar", sq_free)
            av(lambda e: e.activation(rrow[:, :], ps[4][0:1, :], AF.Ln,
                                      scale=1.0 / DM, bias=EPS))
            rr_ev = av2(lambda e: e.activation(rrow_b[:, :], rrow[:, :],
                                               AF.Exp, scale=-0.5))
            bank_r[4].append(rr_ev)
            wev("tensor", rr_ev)
            bank_pre_write(5)
            pe(lambda e: e.matmul(ps[5][:, :], ones_r[:, :],
                                  rrow_b[:, :], start=True, stop=False))
            bc_ev = pe_end(ps[5])
            bank_w[5] = bc_ev
            wev("scalar", bc_ev)
            rstdb_ev = av(lambda e: e.activation(rstdb[:, :], ps[5][:, :],
                                                 AF.Identity), inc=True)
            bank_r[5].append(rstdb_ev)

            gated_evs = {}
            for c in range(KC):
                g = load_piece(P_GATE + c)
                bank = c % 4
                slot = use_piece(g)
                bank_pre_write(bank)
                last = (c == KC - 1)
                for kc in range(KC):
                    pe(lambda e, kc=kc, slot=slot, bank=bank, last=last:
                       e.matmul(
                           ps[bank][:, :], slot[:, kc * 128:(kc + 1) * 128],
                           ln_fm[:, kc * T:(kc + 1) * T],
                           start=(kc == 0),
                           stop=(not last and kc == KC - 1)))
                mm_ev = pe_end(ps[bank]) if last else pe_end_defer()
                bank_w[bank] = mm_ev
                free_piece(g, mm_ev)
                gsl = gate_st[:, (c % 2) * T:(c % 2 + 1) * T]
                r_gst.acquire(w, "scalar", c)
                wev("scalar", mm_ev)
                sig_ev = av(lambda e, gsl=gsl, bank=bank: e.activation(
                    gsl, ps[bank][:, :], AF.Sigmoid), inc=True)
                bank_r[bank].append(sig_ev)
                wev("vector", rstdb_ev)
                wev("vector", sig_ev)
                dv(lambda e, c=c: e.tensor_mul(
                    af[:, c * T:(c + 1) * T], af[:, c * T:(c + 1) * T],
                    rstdb[:, :]))
                dv(lambda e, c=c: e.tensor_scalar_mul(
                    af[:, c * T:(c + 1) * T], af[:, c * T:(c + 1) * T],
                    nw_s[:, c:c + 1]))
                g_ev = dv2(lambda e, c=c, gsl=gsl: e.tensor_mul(
                    af[:, c * T:(c + 1) * T], af[:, c * T:(c + 1) * T], gsl))
                gated_evs[c] = g_ev
                r_gst.release(c, *g_ev)
            lnfm_free_evs.append(mm_ev)

            if stop_after == "gate":
                lnfm_free_evs.append(mm_ev)
                continue

            y_bs = {}
            for fg in range(4):
                for kp in range(4):
                    g = load_piece(P_OUT + fg * 4 + kp)
                    slot = use_piece(g)
                    p_ev = None
                    for kc in range(4):
                        K = kp * 4 + kc
                        wev("tensor", gated_evs[K])
                        for t in range(TC):
                            if kp == 0 and kc == 0:
                                bank_pre_write(t)
                            p_ev = pe(lambda e, K=K, t=t, kc=kc, slot=slot:
                                      e.matmul(
                                          ps[t][:, :],
                                          af[:, K * T + t * 128:
                                             K * T + (t + 1) * 128],
                                          slot[:, kc * 512:(kc + 1) * 512],
                                          start=(kp == 0 and kc == 0),
                                          stop=False),
                                      inc=(kc == 3 and t == TC - 1))
                    free_piece(g, p_ev)
                    if kp == 3:
                        grp_ev = pe_end(ps[TC - 1])
                        for t in range(TC):
                            bank_w[t] = grp_ev
                for t in range(TC):
                    wev("scalar", bank_w[t])
                    yev = av(lambda e, t=t, fg=fg: e.activation(
                        qy[:, t * DM + fg * 512:t * DM + (fg + 1) * 512],
                        ps[t][:, :], AF.Identity), inc=True)
                    bank_r[t].append(yev)
                    wev("vector", yev)
                    y_bs[t] = dv(lambda e, t=t, fg=fg: e.bn_stats(
                        bn6[:, t * 24 + fg * 6:t * 24 + (fg + 1) * 6],
                        qy[:, t * DM + fg * 512:t * DM + (fg + 1) * 512]),
                        inc=(fg == 3))

            lny = {}
            for tc in range(TC):
                lny[tc] = ln_stats(1, tc, y_bs[tc], reg=tc)
            yap_free = None
            new_xs_guard = []
            for tc in range(TC):
                xsl = x_sb[:, (tc % 2) * DM:(tc % 2 + 1) * DM]
                r_x.acquire(w, "scalar", tc)
                xev = dma("scalar", xsl,
                          xin[r0 + tc * 128:r0 + (tc + 1) * 128, :],
                          s[f"xld{tc % 2}"])
                sc, bi, st_ev = lny[tc]
                wev("scalar", st_ev)
                wev("scalar", yap_free)
                yap_ev = av(lambda e, tc=tc, sc=sc, bi=bi: e.activation(
                    y_ap[:, :], qy[:, tc * DM:(tc + 1) * DM], AF.Identity,
                    scale=sc, bias=bi), inc=True)
                mom_free[(1, tc)] = yap_ev
                wev("vector", yap_ev)
                wev("vector", xev)
                resid_ev = dv(lambda e, xsl=xsl: e.tensor_add(
                    xsl, xsl, y_ap[:, :]), inc=True)
                yap_free = resid_ev
                bs = None
                for q in range(4):
                    bs = dv(lambda e, q=q, xsl=xsl, tc=tc: e.bn_stats(
                        bn6[:, tc * 24 + q * 6:tc * 24 + (q + 1) * 6],
                        xsl[:, q * 512:(q + 1) * 512]), inc=(q == 3))
                sc2, bi2, st2_ev = ln_stats(2, tc, bs, reg=tc)
                lsl = ln_st[:, (tc % 2) * DM:(tc % 2 + 1) * DM]
                r_lnst.acquire(w, "scalar", tc)
                wev("scalar", st2_ev)
                ap2_ev = av(lambda e, lsl=lsl, xsl=xsl, sc2=sc2, bi2=bi2:
                            e.activation(lsl, xsl, AF.Identity, scale=sc2,
                                         bias=bi2), inc=True)
                mom_free[(2, tc)] = ap2_ev
                wev("scalar", ap2_ev)
                w.wait("scalar", s["ln_fmd"], lnfm_val)
                lev = dma("scalar", b_ln[tc * 128:(tc + 1) * 128, :], lsl,
                          s[f"lnst{tc % 2}"])
                r_lnst.release(tc, *lev)
                wev("scalar", resid_ev)
                for xg in xs_guard:
                    wev("scalar", xg)
                xsev = dma("scalar", xs[tc * 128:(tc + 1) * 128, :], xsl,
                           s[f"xs{tc % 2}"])
                r_x.release(tc, *xsev)

            for ev in lnfm_free_evs:
                wev("scalar", ev)
            lnfm_free_evs = []
            w.wait("scalar", s["lnst0"])
            w.wait("scalar", s["lnst1"])
            lnfm_base = s["ln_fmd"].n
            for c in range(KC):
                dma("scalar", ln_fm[:, c * T:(c + 1) * T],
                    b_ln[:, c * 128:(c + 1) * 128], s["ln_fmd"], transpose=True)
            lnfm_val = s["ln_fmd"].n
            lnfm_prev = lnfm_val

            if stop_after == "ln2":
                continue

            for m in range(MC_G):
                g = load_piece(P_G + m)
                bank = m % 4
                adj_ev, ast_ev = adjust_piece(j, g, P_G + m)
                slot = use_piece(g)
                wev("tensor", adj_ev)
                bank_pre_write(bank)
                last = (m == MC_G - 1)
                for kc in range(KC):
                    w.wait("tensor", s["ln_fmd"], lnfm_base + (kc + 1) * 16)
                    pe(lambda e, kc=kc, slot=slot, bank=bank, last=last:
                       e.matmul(
                           ps[bank][:, :], slot[:, kc * 128:(kc + 1) * 128],
                           ln_fm[:, kc * T:(kc + 1) * T],
                           start=(kc == 0),
                           stop=(not last and kc == KC - 1)))
                mm_ev = pe_end(ps[bank]) if last else pe_end_defer()
                bank_w[bank] = mm_ev
                free_piece_multi(g, [mm_ev, ast_ev])
                wev("scalar", mm_ev)
                gev = av(lambda e, m=m, bank=bank: e.activation(
                    gu_buf[:, m * T:(m + 1) * T], ps[bank][:, :], AF.Silu,
                    scale=sg_s[:, m:m + 1]), inc=True)
                bank_r[bank].append(gev)
            gu_rdy = {}
            for m in range(MC_G):
                g = load_piece(P_U + m)
                bank = m % 4
                adj_ev, ast_ev = adjust_piece(j, g, P_U + m)
                slot = use_piece(g)
                wev("tensor", adj_ev)
                bank_pre_write(bank)
                last = (m == MC_G - 1) or (j == 0)
                for kc in range(KC):
                    pe(lambda e, kc=kc, slot=slot, bank=bank, last=last:
                       e.matmul(
                           ps[bank][:, :], slot[:, kc * 128:(kc + 1) * 128],
                           ln_fm[:, kc * T:(kc + 1) * T],
                           start=(kc == 0),
                           stop=(not last and kc == KC - 1)))
                mm_ev = pe_end(ps[bank]) if last else pe_end_defer()
                bank_w[bank] = mm_ev
                free_piece_multi(g, [mm_ev, ast_ev])
                wev("vector", mm_ev)
                dve_guard()
                uev = dv2(lambda e, m=m, bank=bank: e.scalar_tensor_tensor(
                    gu_buf[:, m * T:(m + 1) * T], ps[bank][:, :],
                    su_s[:, m:m + 1], gu_buf[:, m * T:(m + 1) * T],
                    ALU.mult, ALU.mult))
                bank_r[bank].append(uev)
                gu_rdy[m] = uev

            f_bs = {}
            for fg in range(4):
                for kp in range(11):
                    g = load_piece(P_DOWN + fg * 11 + kp)
                    adj_ev, ast_ev = adjust_piece(j, g, P_DOWN + fg * 11 + kp)
                    slot = use_piece(g)
                    wev("tensor", adj_ev)
                    p_ev = None
                    for kc in range(4):
                        K = kp * 4 + kc
                        wev("tensor", gu_rdy[K])
                        for t in range(TC):
                            if kp == 0 and kc == 0:
                                bank_pre_write(t)
                            p_ev = pe(lambda e, K=K, t=t, kc=kc, slot=slot:
                                      e.matmul(
                                          ps[t][:, :],
                                          gu_buf[:, K * T + t * 128:
                                                 K * T + (t + 1) * 128],
                                          slot[:, kc * 512:(kc + 1) * 512],
                                          start=(kp == 0 and kc == 0),
                                          stop=False),
                                      inc=(kc == 3 and t == TC - 1))
                    free_piece_multi(g, [p_ev, ast_ev])
                    if kp == 10:
                        grp_ev = pe_end(ps[TC - 1])
                        for t in range(TC):
                            bank_w[t] = grp_ev
                for t in range(TC):
                    wev("scalar", bank_w[t])
                    fev = av(lambda e, t=t, fg=fg: e.activation(
                        ln_fm[:, t * DM + fg * 512:t * DM + (fg + 1) * 512],
                        ps[t][:, :], AF.Identity), inc=True)
                    bank_r[t].append(fev)
                    wev("vector", fev)
                    f_bs[t] = dv(lambda e, t=t, fg=fg: e.bn_stats(
                        bn6[:, t * 24 + fg * 6:t * 24 + (fg + 1) * 6],
                        ln_fm[:, t * DM + fg * 512:t * DM + (fg + 1) * 512]),
                        inc=(fg == 3))

            lnf = {}
            for tc in range(TC):
                lnf[tc] = ln_stats(3, tc, f_bs[tc], reg=tc)
            fap_evs_prev = {}
            for tc in range(TC):
                xsl = x_sb[:, (tc % 2) * DM:(tc % 2 + 1) * DM]
                r_x.acquire(w, "scalar", tc)
                w.wait("scalar", s["xs0"], s["xs0"].n)
                w.wait("scalar", s["xs1"], s["xs1"].n)
                xev = dma("scalar", xsl, xs[tc * 128:(tc + 1) * 128, :],
                          s[f"xld{tc % 2}"])
                sc3, bi3, st3_ev = lnf[tc]
                wev("scalar", st3_ev)
                wev("scalar", yap_free)
                fap_ev = av(lambda e, tc=tc, sc3=sc3, bi3=bi3: e.activation(
                    y_ap[:, :], ln_fm[:, tc * DM:(tc + 1) * DM], AF.Identity,
                    scale=sc3, bias=bi3), inc=True)
                mom_free[(3, tc)] = fap_ev
                fap_evs_prev[tc] = fap_ev
                wev("vector", fap_ev)
                wev("vector", xev)
                fr_ev = dv(lambda e, xsl=xsl: e.tensor_add(
                    xsl, xsl, y_ap[:, :]), inc=True)
                yap_free = fr_ev
                wev("scalar", fr_ev)
                oev = dma("scalar",
                          out_tm[r0 + tc * 128:r0 + (tc + 1) * 128, :], xsl,
                          s[f"out{tc % 2}"])
                r_x.release(tc, *oev)
                new_xs_guard.append(xev)
            xs_guard = new_xs_guard

        w.wait("scalar", s["out0"], s["out0"].n)
        w.wait("scalar", s["out1"], s["out1"].n)
        w.wait("gpsimd", s["pre_st0"], s["pre_st0"].n)
        w.wait("gpsimd", s["pre_st1"], s["pre_st1"].n)

        tap_src = {"ln_fm": ln_fm, "q_fm": qy, "k_fm": k_fm, "v_tm": v_tm,
                   "k_tm": k_tm, "attn": af, "y_tm": qy, "gu": gu_buf,
                   "f_tm": ln_fm, "rstdb": rstdb, "S": S_all}
        if taps:
            for nm_ in ["ln_fmd", "kv_st", "vst0", "vst1", "ktm", "vtm",
                        "misc", "pe", "act", "dvec", "pool"]:
                w.wait("gpsimd", s[nm_], s[nm_].n)
        for nm, ap in taps.items():
            src = tap_src[nm]
            dma("gpsimd", ap, src[:, 0:ap.shape[-1]], s["tapd"])
        w.wait("gpsimd", s["tapd"], s["tapd"].n)

    with nc.Block() as block:
        for name in ["sync", "scalar", "vector", "tensor", "gpsimd"]:
            def make(nm):
                def f(eng):
                    schedule(Walk(nm, eng, list(SM.values())))
                return f
            getattr(block, name)(make(name))

    ctx.close()
    return nc



def _tables():
    base = 1.0 / 2.0 ** (8.0 / H)
    rate = base ** (np.arange(H, dtype=np.float64) + 1.0)
    factor = 1.0 - LAYER_IDX / (NUM_LAYERS - 1 + 1e-5) + 1e-5
    slope = (rate * factor)[:, None]
    i = np.arange(BS, dtype=np.float64)[None, :]
    qd = np.exp(-slope * (i + 1.0))
    kd = np.exp(-slope * (BS - (i + 1.0)))
    dif = i[0][:, None] - i[0][None, :]
    ddm = np.exp(-slope[:, None] * np.maximum(dif, 0.0)) * (dif >= 0)
    qd_t = np.zeros((128, KC * BS), np.float64)
    for c in range(KC):
        for half in range(2):
            qd_t[half * 64:(half + 1) * 64, c * BS:(c + 1) * BS] = qd[2 * c + half]
    kd_t = np.zeros((128, 2 * H), np.float64)
    for jc in range(2):
        for h in range(H):
            kd_t[:, jc * H + h] = kd[h, jc * 128:(jc + 1) * 128]
    ddT = np.zeros((H, 128, 2 * BS), np.float64)
    for jc in range(2):
        ddT[:, :, jc * BS:(jc + 1) * BS] = np.transpose(
            ddm[:, :, jc * 128:(jc + 1) * 128], (0, 2, 1))
    return (np.ascontiguousarray(qd_t.astype(ml_dtypes.bfloat16)),
            np.ascontiguousarray(kd_t.astype(np.float32)),
            np.ascontiguousarray(ddT.astype(ml_dtypes.bfloat16)))


def _prep_shared(inp):
    qkv_w = np.asarray(inp["L0_attn_qkv_w"])
    idx = np.arange(3 * DM)
    part, rem = idx // DM, idx % DM
    perm = (rem // D) * (3 * D) + part * D + (rem % D)
    qkvT = qkv_w[perm].T.astype(np.float32)

    def mtile(wT):
        K_, M_ = wT.shape
        return np.ascontiguousarray(
            wT.reshape(K_ // 128, 128, M_ // 128, 128)
            .transpose(2, 1, 0, 3).reshape(M_ // 128 * 128, K_ // 128 * 128)
            .reshape(M_ // 128 * 128, K_))

    def ftile(wT):
        K_, M_ = wT.shape
        a = wT.reshape(K_ // 512, 4, 128, M_ // 512, 512)
        a = a.transpose(3, 0, 2, 1, 4)
        return np.ascontiguousarray(a.reshape(M_ // 512 * (K_ // 512) * 128,
                                              2048))

    qd_t, kd_t, ddT = _tables()
    return dict(
        qkvT=mtile(qkvT),
        gateT=mtile(np.asarray(inp["L0_attn_gate_w"]).T.astype(np.float32)),
        outT=ftile(np.asarray(inp["L0_attn_out_w"]).T.astype(np.float32)),
        wqgT=mtile(np.asarray(inp["L0_ffn_wq_gate"]).T.astype(np.int32)),
        wquT=mtile(np.asarray(inp["L0_ffn_wq_up"]).T.astype(np.int32)),
        wqdT=ftile(np.asarray(inp["L0_ffn_wq_down"]).T.astype(np.int32)),
        sg_p=np.ascontiguousarray(
            np.asarray(inp["L0_ffn_s_gate"]).astype(np.float32).reshape(MC_G, 128).T),
        su_p=np.ascontiguousarray(
            np.asarray(inp["L0_ffn_s_up"]).astype(np.float32).reshape(MC_G, 128).T),
        sd_bc=np.ascontiguousarray(np.broadcast_to(
            np.asarray(inp["L0_ffn_s_down"]).astype(np.float32)[None, :], (128, DM))),
        nw_p=np.ascontiguousarray(
            np.asarray(inp["L0_attn_norm_w"]).astype(np.float32).reshape(KC, 128).T),
        qd_t=qd_t, kd_t=kd_t, ddT_t=ddT)


def kernel(**inputs):
    x = np.asarray(inputs["L0_attn_hidden_states"])
    shared = _prep_shared(inputs)
    nc = build_program(n_tiles=8)
    in_maps = []
    for c in range(NCORE):
        m = dict(shared)
        m["xin"] = np.ascontiguousarray(
            x[2 * c:2 * c + 2].reshape(8 * T, DM).astype(np.float32))
        in_maps.append(m)
    res = run_bass_kernel_spmd(nc, in_maps, core_ids=list(range(NCORE)))
    outs = [r["out_tm"].reshape(2, 2048, DM) for r in res.results]
    return np.concatenate(outs, axis=0)


# version 11
# speedup vs baseline: 1.0766x; 1.0284x over previous
import sys

sys.path.insert(0, "/opt/trn_rl_repo")

import numpy as np
import ml_dtypes

import concourse.bass as bass
import concourse.mybir as mybir
from concourse.bass_utils import run_bass_kernel_spmd
import concourse.bass_utils as _bass_utils

_orig_run_command = _bass_utils.run_command


def _patched_run_command(cmd, **kw):
    if isinstance(cmd, list):
        cmd = [("--enable-ldw-opt=true" if c == "--enable-ldw-opt=false" else c)
               for c in cmd]
    return _orig_run_command(cmd, **kw)


_bass_utils.run_command = _patched_run_command

F32 = mybir.dt.float32
BF16 = mybir.dt.bfloat16
I32 = mybir.dt.int32
AF = mybir.ActivationFunctionType
ALU = mybir.AluOpType

H, D, BS = 32, 64, 256
DM = 2048
HID = 5632
T = 512
NB = 2
EPS = 1e-5
KC = DM // 128
MC_QKV = 3 * DM // 128
MC_G = HID // 128
TC = T // 128
NCORE = 8
LAYER_IDX, NUM_LAYERS = 0, 32


class Sem:
    __slots__ = ("name", "h", "n")

    def __init__(self, name):
        self.name, self.h, self.n = name, None, 0


class Walk:

    def __init__(self, who, eng, sems):
        self.who = who
        self.eng = eng
        for s in sems:
            s.n = 0
        self.last_wait = {}

    def on(self, name):
        return self.who == name

    def wait(self, who, sem, val=None):
        v = sem.n if val is None else val
        if self.who == who and v > 0:
            if v > self.last_wait.get(sem.name, 0):
                self.eng.wait_ge(sem.h, v)
                self.last_wait[sem.name] = v

    def emit(self, who, build, incs=()):
        ins = build(self.eng) if self.who == who else None
        for i, (sem, k) in enumerate(incs):
            sem.n += k
            if ins is not None:
                if i == 0:
                    ins.then_inc(sem.h, k)
                else:
                    self.eng.nop().then_inc(sem.h, k)
        return ins

    def mark(self, who, sem, k=1):
        sem.n += k
        if self.who == who:
            self.eng.nop().then_inc(sem.h, k)


class Ring:
    def __init__(self, nslots):
        self.n = nslots
        self.free = [None] * nslots

    def acquire(self, w, who, idx):
        evt = self.free[idx % self.n]
        if evt is not None:
            for sem, val in evt:
                w.wait(who, sem, val)

    def release(self, idx, sem, val=None):
        self.free[idx % self.n] = [(sem, sem.n if val is None else val)]

    def release_multi(self, idx, evs):
        self.free[idx % self.n] = [e for e in evs if e is not None]


def build_program(n_tiles=8, stop_after=None, debug_taps=()):
    nc = bass.Bass("TRN2", target_bir_lowering=False)

    xin = nc.dram_tensor("xin", [8 * T, DM], F32, kind="ExternalInput").ap()
    qkvT = nc.dram_tensor("qkvT", [MC_QKV * 128, 2048], F32, kind="ExternalInput").ap()
    gateT = nc.dram_tensor("gateT", [KC * 128, 2048], F32, kind="ExternalInput").ap()
    outT = nc.dram_tensor("outT", [16 * 128, 2048], F32, kind="ExternalInput").ap()
    wqgT = nc.dram_tensor("wqgT", [MC_G * 128, 2048], I32, kind="ExternalInput").ap()
    wquT = nc.dram_tensor("wquT", [MC_G * 128, 2048], I32, kind="ExternalInput").ap()
    wqdT = nc.dram_tensor("wqdT", [44 * 128, 2048], I32, kind="ExternalInput").ap()
    sg_in = nc.dram_tensor("sg_p", [128, MC_G], F32, kind="ExternalInput").ap()
    su_in = nc.dram_tensor("su_p", [128, MC_G], F32, kind="ExternalInput").ap()
    sd_in = nc.dram_tensor("sd_bc", [128, DM], F32, kind="ExternalInput").ap()
    nw_in = nc.dram_tensor("nw_p", [128, KC], F32, kind="ExternalInput").ap()
    qd_in = nc.dram_tensor("qd_t", [128, KC * BS], BF16, kind="ExternalInput").ap()
    kd_in = nc.dram_tensor("kd_t", [128, 2 * H], F32, kind="ExternalInput").ap()
    dd_in = nc.dram_tensor("ddT_t", [H, 128, 2 * BS], BF16, kind="ExternalInput").ap()

    out_tm = nc.dram_tensor("out_tm", [8 * T, DM], F32, kind="ExternalOutput").ap()

    ws_qkv = nc.dram_tensor("ws_qkv", [MC_QKV * 128, 2048], BF16, kind="Internal").ap()
    ws_gate = nc.dram_tensor("ws_gate", [KC * 128, 2048], BF16, kind="Internal").ap()
    ws_g = nc.dram_tensor("ws_g", [MC_G * 128, 2048], BF16, kind="Internal").ap()
    ws_u = nc.dram_tensor("ws_u", [MC_G * 128, 2048], BF16, kind="Internal").ap()
    ws_out = nc.dram_tensor("ws_out", [16 * 128, 2048], BF16, kind="Internal").ap()
    ws_down = nc.dram_tensor("ws_down", [44 * 128, 2048], BF16, kind="Internal").ap()
    b_ln = nc.dram_tensor("b_ln", [T, DM], BF16, kind="Internal").ap()
    b_k = nc.dram_tensor("b_k", [DM, T], BF16, kind="Internal").ap()
    b_v = nc.dram_tensor("b_v", [DM, T], BF16, kind="Internal").ap()
    xs = nc.dram_tensor("xs", [T, DM], F32, kind="Internal").ap()

    tap_shapes = {
        "ln_fm": ([128, KC * T], BF16), "q_fm": ([128, KC * T], BF16),
        "k_fm": ([128, KC * T], BF16), "v_tm": ([128, 2 * DM], BF16),
        "k_tm": ([128, 2 * DM], BF16), "attn": ([128, KC * T], BF16),
        "y_tm": ([128, TC * DM], BF16), "gu": ([128, MC_G * T], BF16),
        "f_tm": ([128, KC * T], BF16), "rstdb": ([128, T], BF16),
        "S": ([64, H * D], F32),
    }
    taps = {nm: nc.dram_tensor("tap_" + nm, *tap_shapes[nm], kind="ExternalOutput").ap()
            for nm in debug_taps}

    from contextlib import ExitStack
    ctx = ExitStack()
    sb = lambda name, shape, dt: ctx.enter_context(nc.sbuf_tensor(name, shape, dt))

    x_sb = sb("x_sb", [128, 2 * DM], F32)
    ln_st = sb("ln_st", [128, 2 * DM], BF16)
    ln_fm = sb("ln_fm", [128, KC * T], BF16)
    qy = sb("qy", [128, KC * T], BF16)
    k_fm = sb("k_fm", [128, KC * T], BF16)
    v_st = sb("v_st", [128, 2 * T], BF16)
    k_tm = sb("k_tm", [128, 2 * DM], BF16)
    v_tm = sb("v_tm", [128, 2 * DM], BF16)
    af = sb("af", [128, KC * T], BF16)
    gu_buf = sb("gu_buf", [128, MC_G * T], BF16)
    gate_st = sb("gate_st", [128, 2 * T], BF16)
    y_ap = sb("y_ap", [128, DM], BF16)
    ring_t = sb("ring_t", [128, 3 * 2048], BF16)
    S_all = sb("S_all", [64, H * D], F32)
    sdb = sb("sdb", [128, DM], BF16)
    qd_t = sb("qd_ts", [128, KC * BS], BF16)
    kd_t = sb("kd_ts", [128, 2 * H], F32)
    dd_b = sb("dd_b", [128, 2 * 2 * BS], BF16)
    awd = sb("awd", [128, 2 * 2 * BS], BF16)
    sq_st = sb("sq_st", [128, T], BF16)
    rstdb = sb("rstdb", [128, T], BF16)
    nw_s = sb("nw_s", [128, KC], F32)
    sg_s = sb("sg_s", [128, MC_G], F32)
    su_s = sb("su_s", [128, MC_G], F32)
    qqd = sb("qqd", [64, BS], F32)
    bn6 = sb("bn6", [128, 4 * 24], F32)
    mom = sb("mom", [128, 4 * 16], F32)
    rrow = sb("rrow", [1, T], F32)
    rrow_b = sb("rrow_b", [1, T], BF16)
    ones_c = sb("ones_c", [128, 1], BF16)
    ones_r = sb("ones_r", [1, 128], BF16)
    zc = sb("zc", [1, 2], BF16)
    cst = sb("cst", [128, 4], F32)
    nc.const_aps.register(F32, EPS, cst[:, 0:1])
    nc.const_aps.register(F32, -0.5, cst[:, 1:2])
    nc.const_aps.register(F32, 1.0 / DM, cst[:, 2:3])

    assert nc.sbuf_bytes_remaining >= 0, "SBUF over budget"

    ps = [ctx.enter_context(nc.psum_tensor(f"ps{i}", [128, 512], F32))
          for i in range(8)]

    names = (["pre_st0", "pre_st1", "wp0", "wp1", "wp2", "dd0", "dd1",
              "xld0", "xld1", "lnst0", "lnst1", "ln_fmd",
              "kv_st", "vst0", "vst1", "ktm", "vtm",
              "xs0", "xs1", "out0", "out1", "misc", "tapd",
              "pe", "act", "dvec", "pool", "ast0", "ast1", "ast2"])
    SM = {n: Sem(n) for n in names}
    for s_ in SM.values():
        s_.h = nc.alloc_semaphore(s_.name)

    def scr2(ws, idx):
        return ws[idx * 128:(idx + 1) * 128, :]

    pieces = []
    for m in range(MC_QKV):
        pieces.append((scr2(ws_qkv, m), scr2(qkvT, m), None, 128))
    for m in range(KC):
        pieces.append((scr2(ws_gate, m), scr2(gateT, m), None, 128))
    for i in range(16):
        pieces.append((scr2(ws_out, i), scr2(outT, i), None, 512))
    for m in range(MC_G):
        pieces.append((scr2(ws_g, m), scr2(wqgT, m), "m128", 128))
    for m in range(MC_G):
        pieces.append((scr2(ws_u, m), scr2(wquT, m), "m128", 128))
    for i in range(44):
        fg = i // 11
        pieces.append((scr2(ws_down, i), scr2(wqdT, i), ("down", fg), 512))
    NP = len(pieces)
    P_QKV, P_GATE, P_OUT, P_G, P_U, P_DOWN = 0, 48, 64, 80, 124, 168

    base = 1.0 / 2.0 ** (8.0 / H)
    rate = base ** (np.arange(H, dtype=np.float64) + 1.0)
    factor = 1.0 - LAYER_IDX / (NUM_LAYERS - 1 + 1e-5) + 1e-5
    bd_py = np.exp(-rate * factor * BS)

    def schedule(w: Walk):
        s = SM
        EC = {"tensor": s["pe"], "scalar": s["act"], "vector": s["dvec"],
              "gpsimd": s["pool"]}

        def dma(who, out, in_, sem, **kw):
            w.emit(who, lambda e: e.dma_start(out, in_, **kw), [(sem, 16)])
            return (sem, sem.n)

        def comp(who, build, inc=False):
            w.emit(who, build, [(EC[who], 1)] if inc else [])
            return (EC[who], EC[who].n) if inc else None

        pend_end = [None]

        def pe(build, inc=False):
            if pend_end[0] is not None:
                pend_end[0] = None
                inc = True
            return comp("tensor", build, inc)

        def av(build, inc=False):
            return comp("scalar", build, inc)

        def dv(build, inc=False):
            return comp("vector", build, inc)

        def wev(who, ev):
            if ev is not None:
                w.wait(who, ev[0], ev[1])

        def pe_end(bank_ap):
            assert pend_end[0] is None
            return pe(lambda e: e.matmul(bank_ap[0:1, 0:128], zc[0:1, 0:1],
                                         ones_r[0:1, 0:128], start=False,
                                         stop=True), inc=True)

        def pe_end_defer():
            assert pend_end[0] is None
            pend_end[0] = True
            return (EC["tensor"], EC["tensor"].n + 1)

        def dv2(build):
            dv(build)
            return dv(lambda e: e.memset(cst[:, 3:4], 0.0), inc=True)

        def av2(build):
            av(build)
            return av(lambda e: e.activation(cst[0:1, 3:4], cst[0:1, 3:4],
                                             AF.Identity), inc=True)

        def dve_guard():
            dv(lambda e: e.memset(cst[:, 3:4], 0.0))

        bank_w = [None] * 8
        bank_r = [[] for _ in range(8)]

        def bank_pre_write(i, who="tensor"):
            for ev in bank_r[i]:
                wev(who, ev)
            bank_r[i] = []

        def bank_read(i, who, ev=None):
            wev(who, bank_w[i])

        r_tile, r_x, r_lnst = Ring(3), Ring(2), Ring(2)
        r_vst, r_gst, r_dd, r_awd = Ring(2), Ring(2), Ring(2), Ring(1)
        hbs_last = {}
        mom_free = {}
        qk_rdy = {}
        attn_evac = {}
        lnfm_free_evs = []
        qqd_free = None

        for dst, srcin in [(qd_t[:, :], qd_in), (kd_t[:, :], kd_in),
                           (nw_s[:, :], nw_in), (sg_s[:, :], sg_in),
                           (su_s[:, :], su_in)]:
            dma("gpsimd", dst, srcin, s["misc"])
        dma("gpsimd", sdb[:, :], sd_in, s["misc"])
        sdb_ld = (s["misc"], s["misc"].n)
        dv(lambda e: e.memset(cst[:, 0:1], EPS))
        dv(lambda e: e.memset(cst[:, 1:2], -0.5))
        dv(lambda e: e.memset(cst[:, 2:3], 1.0 / DM))
        dv(lambda e: e.memset(ones_c[:, :], 1.0))
        dv(lambda e: e.memset(ones_r[:, :], 1.0))
        dv(lambda e: e.memset(zc[:, :], 0.0))
        init_dve = dv(lambda e: e.memset(S_all[:, :], 0.0), inc=True)
        wev("scalar", init_dve)
        wev("tensor", init_dve)
        w.wait("tensor", s["misc"])

        pre_ev = {}
        for p_i in range(NP):
            scr, hsrc, adj, wdt = pieces[p_i]
            pre_ev[p_i] = dma("gpsimd", scr, hsrc, s["pre_st0"])

        for eng_ in ("sync", "scalar", "vector", "tensor", "gpsimd"):
            w.wait(eng_, s["pre_st0"], s["pre_st0"].n)
            w.wait(eng_, s["pre_st1"], s["pre_st1"].n)

        pc = [0]
        piece_rdy, piece_slot = {}, {}

        def load_piece(p_i):
            g = pc[0]
            slot = ring_t[:, (g % 3) * 2048:(g % 3 + 1) * 2048]
            r_tile.acquire(w, "sync", g)
            wev("sync", pre_ev[p_i])
            piece_rdy[g] = dma("sync", slot, pieces[p_i][0], s[f"wp{g % 3}"])
            piece_slot[g] = slot
            pc[0] += 1
            return g

        def use_piece(g):
            wev("tensor", piece_rdy[g])
            return piece_slot[g]

        def free_piece(g, ev):
            r_tile.release(g, ev[0], ev[1])

        def free_piece_multi(g, evs):
            r_tile.release_multi(g, evs)

        def adjust_piece(j, g, p_i):
            if j != 0:
                return None, None
            slot = piece_slot[g]
            adj = pieces[p_i][2]
            wev("vector", piece_rdy[g])
            adj_ev = None
            if adj == "m128":
                adj_ev = dv(lambda e, sl=slot: e.tensor_scalar_add(
                    sl, sl, -128.0), inc=True)
            else:
                fg = adj[1]
                wev("vector", sdb_ld)
                dv(lambda e, sl=slot: e.tensor_scalar_add(sl, sl, -128.0))
                for kb in range(4):
                    adj_ev = dv(lambda e, sl=slot, kb=kb, fg=fg:
                                e.tensor_mul(sl[:, kb * 512:(kb + 1) * 512],
                                             sl[:, kb * 512:(kb + 1) * 512],
                                             sdb[:, fg * 512:(fg + 1) * 512]),
                                inc=(kb == 3))
            wev("scalar", adj_ev)
            st_ev = dma("scalar", pieces[p_i][0], slot, s[f"ast{g % 3}"])
            pre_ev[p_i] = st_ev
            return adj_ev, st_ev

        def ln_stats(site, tc, bs_ev, reg=0):
            mo = mom[:, site * 16 + tc * 4: site * 16 + tc * 4 + 4]
            evt = mom_free.get((site, tc))
            if evt is not None:
                wev("vector", evt)
            wev("vector", bs_ev)
            aggr_ev = dv(lambda e: e.bn_aggr(
                mo[:, 0:2], bn6[:, reg * 24:(reg + 1) * 24]), inc=True)
            wev("scalar", aggr_ev)
            av(lambda e: e.activation(mo[:, 2:3], mo[:, 1:2], AF.Ln, bias=EPS))
            exp_ev = av(lambda e: e.activation(mo[:, 2:3], mo[:, 2:3], AF.Exp,
                                               scale=-0.5), inc=True)
            wev("vector", exp_ev)
            neg_ev = dv(lambda e: e.scalar_tensor_tensor(
                mo[:, 3:4], mo[:, 0:1], -1.0, mo[:, 2:3],
                ALU.mult, ALU.mult), inc=True)
            return mo[:, 2:3], mo[:, 3:4], neg_ev

        vtm_prev = ktm_prev = lnfm_prev = 0
        fap_evs_prev = {}
        xs_guard = []
        att_pe_ev = None
        for j in range(n_tiles):
            r0 = j * T

            if j == 4:
                wev("vector", att_pe_ev)
                ms_ev = dv(lambda e: e.memset(S_all[:, :], 0.0), inc=True)
                for h in range(H):
                    hbs_last[h] = ms_ev

            for tc in range(TC):
                xsl = x_sb[:, (tc % 2) * DM:(tc % 2 + 1) * DM]
                r_x.acquire(w, "scalar", tc)
                xev = dma("scalar", xsl,
                          xin[r0 + tc * 128:r0 + (tc + 1) * 128, :],
                          s[f"xld{tc % 2}"])
                wev("vector", xev)
                bs_ev = None
                for q in range(4):
                    bs_ev = dv(lambda e, q=q, xsl=xsl, tc=tc: e.bn_stats(
                        bn6[:, tc * 24 + q * 6:tc * 24 + (q + 1) * 6],
                        xsl[:, q * 512:(q + 1) * 512]), inc=(q == 3))
                sc, bi, st_ev = ln_stats(0, tc, bs_ev, reg=tc)
                lsl = ln_st[:, (tc % 2) * DM:(tc % 2 + 1) * DM]
                r_lnst.acquire(w, "scalar", tc)
                wev("scalar", st_ev)
                ap_ev = av(lambda e, lsl=lsl, xsl=xsl, sc=sc, bi=bi: e.activation(
                    lsl, xsl, AF.Identity, scale=sc, bias=bi), inc=True)
                r_x.release(tc, *ap_ev)
                mom_free[(0, tc)] = ap_ev
                wev("scalar", ap_ev)
                w.wait("scalar", s["ln_fmd"], lnfm_prev)
                lev = dma("scalar", b_ln[tc * 128:(tc + 1) * 128, :], lsl,
                          s[f"lnst{tc % 2}"])
                r_lnst.release(tc, *lev)

            w.wait("scalar", s["lnst0"])
            w.wait("scalar", s["lnst1"])
            lnfm_base = s["ln_fmd"].n
            for c in range(KC):
                if fap_evs_prev:
                    wev("scalar", fap_evs_prev[c // 4])
                dma("scalar", ln_fm[:, c * T:(c + 1) * T],
                    b_ln[:, c * 128:(c + 1) * 128], s["ln_fmd"], transpose=True)
            lnfm_val = s["ln_fmd"].n

            for mi in range(MC_QKV):
                m = [*range(32, 48), *range(16, 32), *range(0, 16)][mi]
                g = load_piece(P_QKV + m)
                bank = mi % 4
                slot = use_piece(g)
                bank_pre_write(bank)
                last = (mi == MC_QKV - 1)
                for kc in range(KC):
                    w.wait("tensor", s["ln_fmd"], lnfm_base + (kc + 1) * 16)
                    pe(lambda e, kc=kc, slot=slot, bank=bank, last=last:
                       e.matmul(
                           ps[bank][:, :], slot[:, kc * 128:(kc + 1) * 128],
                           ln_fm[:, kc * T:(kc + 1) * T],
                           start=(kc == 0),
                           stop=(not last and kc == KC - 1)))
                mm_ev = pe_end(ps[bank]) if last else pe_end_defer()
                bank_w[bank] = mm_ev
                free_piece(g, mm_ev)
                wev("scalar", mm_ev)
                if m < 16:
                    ev = av2(lambda e, m=m, bank=bank: e.activation(
                        qy[:, m * T:(m + 1) * T], ps[bank][:, :], AF.Silu))
                    bank_r[bank].append(ev)
                    qk_rdy[("q", m)] = ev
                elif m < 32:
                    ev = av2(lambda e, m=m, bank=bank: e.activation(
                        k_fm[:, (m - 16) * T:(m - 15) * T], ps[bank][:, :],
                        AF.Silu))
                    bank_r[bank].append(ev)
                    qk_rdy[("k", m - 16)] = ev
                    wev("scalar", ev)
                    w.wait("scalar", s["ktm"], ktm_prev)
                    dma("scalar", b_k[(m - 16) * 128:(m - 15) * 128, :],
                        k_fm[:, (m - 16) * T:(m - 15) * T], s["kv_st"])
                else:
                    vs = v_st[:, (m % 2) * T:(m % 2 + 1) * T]
                    r_vst.acquire(w, "scalar", m)
                    ev = av(lambda e, vs=vs, bank=bank: e.activation(
                        vs, ps[bank][:, :], AF.Silu), inc=True)
                    bank_r[bank].append(ev)
                    wev("scalar", ev)
                    w.wait("scalar", s["vtm"], vtm_prev)
                    vev = dma("scalar", b_v[(m - 32) * 128:(m - 31) * 128, :],
                              vs, s[f"vst{m % 2}"])
                    r_vst.release(m, *vev)
            kv_done = (s["kv_st"].n, s["vst0"].n, s["vst1"].n)

            if stop_after == "qkv":
                lnfm_free_evs.append(mm_ev)
                continue

            for blk in range(NB):
                wev("scalar", att_pe_ev)
                w.wait("scalar", s["kv_st"], kv_done[0])
                w.wait("scalar", s["vst0"], kv_done[1])
                w.wait("scalar", s["vst1"], kv_done[2])
                for jc in range(2):
                    dma("scalar", k_tm[:, jc * DM:(jc + 1) * DM],
                        b_k[:, blk * 256 + jc * 128:blk * 256 + (jc + 1) * 128],
                        s["ktm"], transpose=True)
                    dma("scalar", v_tm[:, jc * DM:(jc + 1) * DM],
                        b_v[:, blk * 256 + jc * 128:blk * 256 + (jc + 1) * 128],
                        s["vtm"], transpose=True)
                w.wait("vector", s["ktm"])
                wev("vector", att_pe_ev)
                for jc in range(2):
                    for h in range(H):
                        dv(lambda e, jc=jc, h=h: e.tensor_scalar_mul(
                            k_tm[:, jc * DM + h * 64:jc * DM + (h + 1) * 64],
                            k_tm[:, jc * DM + h * 64:jc * DM + (h + 1) * 64],
                            kd_t[:, jc * H + h:jc * H + h + 1]))
                kd_ev = dv(lambda e: e.memset(cst[:, 3:4], 0.0), inc=True)
                vtm_val = s["vtm"].n

                last_state_ev = None
                for h in range(H):
                    hb = (j * NB + blk) * H + h
                    ab, tb = 4 + (h % 2), 6 + (h % 2)
                    hc, hr = h // 2, (h % 2) * 64
                    qcol = hc * T + blk * 256
                    ddl = dd_b[:, (h % 2) * 512:(h % 2 + 1) * 512]
                    r_dd.acquire(w, "scalar", hb)
                    dd_ev = dma("scalar", ddl, dd_in[h], s[f"dd{h % 2}"])
                    wev("vector", qk_rdy[("q", hc)])
                    if hb > 0:
                        wev("vector", qqd_free)
                    qqd_ev = dv2(lambda e, hr=hr, qcol=qcol, hc=hc: e.tensor_mul(
                        qqd[:, :], qy[hr:hr + 64, qcol:qcol + 256],
                        qd_t[hr:hr + 64, hc * 256:(hc + 1) * 256]))
                    wev("tensor", qk_rdy[("k", hc)])
                    wev("tensor", qk_rdy[("q", hc)])
                    bank_pre_write(ab)
                    for jc in range(2):
                        pe(lambda e, jc=jc, hr=hr, qcol=qcol, ab=ab:
                           e.matmul(
                               ps[ab][:, jc * 256:(jc + 1) * 256],
                               k_fm[hr:hr + 64,
                                    qcol + jc * 128:qcol + (jc + 1) * 128],
                               qy[hr:hr + 64, qcol:qcol + 256],
                               start=(jc == 0), stop=False))
                    aw_ev = pe_end(ps[ab])
                    bank_w[ab] = aw_ev
                    asl = awd[:, 0:512]
                    r_awd.acquire(w, "vector", hb)
                    wev("vector", aw_ev)
                    wev("vector", dd_ev)
                    dve_guard()
                    awd_ev = dv2(lambda e, asl=asl, ab=ab, ddl=ddl: e.tensor_mul(
                        asl, ps[ab][:, :], ddl))
                    bank_r[ab].append(awd_ev)
                    r_dd.release(hb, *awd_ev)
                    bank_pre_write(tb)
                    wev("tensor", awd_ev)
                    w.wait("tensor", s["vtm"], vtm_val)
                    for jc in range(2):
                        pe(lambda e, jc=jc, h=h, tb=tb, asl=asl: e.matmul(
                            ps[tb][0:64, 0:256],
                            v_tm[:, jc * DM + h * 64:jc * DM + (h + 1) * 64],
                            asl[:, jc * 256:(jc + 1) * 256],
                            start=(jc == 0), stop=False))
                    wev("tensor", qqd_ev)
                    wev("tensor", hbs_last.get(h))
                    pe(lambda e, h=h, tb=tb: e.matmul(
                        ps[tb][0:64, 0:256], S_all[:, h * 64:(h + 1) * 64],
                        qqd[:, :], start=False, stop=False))
                    at_ev = pe_end(ps[tb])
                    bank_w[tb] = at_ev
                    r_awd.release(hb, *at_ev)
                    qqd_free = at_ev
                    wev("scalar", at_ev)
                    ev_ev = av(lambda e, hr=hr, qcol=qcol, tb=tb: e.activation(
                        af[hr:hr + 64, qcol:qcol + 256], ps[tb][0:64, 0:256],
                        AF.Identity), inc=True)
                    bank_r[tb].append(ev_ev)
                    attn_evac[(j, blk, h)] = ev_ev
                    bank_pre_write(tb)
                    wev("tensor", kd_ev)
                    for jc in range(2):
                        pe(lambda e, jc=jc, h=h, tb=tb: e.matmul(
                            ps[tb][0:64, 0:64],
                            k_tm[:, jc * DM + h * 64:jc * DM + (h + 1) * 64],
                            v_tm[:, jc * DM + h * 64:jc * DM + (h + 1) * 64],
                            start=(jc == 0), stop=False))
                    st_ev = pe_end(ps[tb])
                    bank_w[tb] = st_ev
                    last_state_ev = st_ev
                    wev("vector", st_ev)
                    dve_guard()
                    dv(lambda e, h=h: e.tensor_scalar_mul(
                        S_all[:, h * 64:(h + 1) * 64],
                        S_all[:, h * 64:(h + 1) * 64], float(bd_py[h])))
                    su_ev = dv2(lambda e, h=h, tb=tb: e.tensor_add(
                        S_all[:, h * 64:(h + 1) * 64],
                        S_all[:, h * 64:(h + 1) * 64], ps[tb][0:64, 0:64]))
                    bank_r[tb].append(su_ev)
                    hbs_last[h] = su_ev
                att_pe_ev = last_state_ev
            vtm_prev, ktm_prev = s["vtm"].n, s["ktm"].n

            if stop_after == "attn":
                lnfm_free_evs.append(att_pe_ev)
                continue

            sq_free = None
            for c in range(KC):
                wev("vector", attn_evac[(j, 0, 2 * c + 1)])
                wev("vector", attn_evac[(j, 1, 2 * c + 1)])
                wev("vector", sq_free)
                sq_ev = dv(lambda e, c=c: e.tensor_mul(
                    sq_st[:, :], af[:, c * T:(c + 1) * T],
                    af[:, c * T:(c + 1) * T]), inc=True)
                wev("tensor", sq_ev)
                if c == 0:
                    bank_pre_write(4)
                mm_ev = pe(lambda e, c=c: e.matmul(
                    ps[4][0:1, :], ones_c[:, :], sq_st[:, :],
                    start=(c == 0), stop=False), inc=(c < KC - 1))
                sq_free = mm_ev
            sq_free = pe_end(ps[4])
            bank_w[4] = sq_free
            wev("scalar", sq_free)
            av(lambda e: e.activation(rrow[:, :], ps[4][0:1, :], AF.Ln,
                                      scale=1.0 / DM, bias=EPS))
            rr_ev = av2(lambda e: e.activation(rrow_b[:, :], rrow[:, :],
                                               AF.Exp, scale=-0.5))
            bank_r[4].append(rr_ev)
            wev("tensor", rr_ev)
            bank_pre_write(5)
            pe(lambda e: e.matmul(ps[5][:, :], ones_r[:, :],
                                  rrow_b[:, :], start=True, stop=False))
            bc_ev = pe_end(ps[5])
            bank_w[5] = bc_ev
            wev("scalar", bc_ev)
            rstdb_ev = av(lambda e: e.activation(rstdb[:, :], ps[5][:, :],
                                                 AF.Identity), inc=True)
            bank_r[5].append(rstdb_ev)

            gated_evs = {}
            for c in range(KC):
                g = load_piece(P_GATE + c)
                bank = c % 4
                slot = use_piece(g)
                bank_pre_write(bank)
                last = (c == KC - 1)
                for kc in range(KC):
                    pe(lambda e, kc=kc, slot=slot, bank=bank, last=last:
                       e.matmul(
                           ps[bank][:, :], slot[:, kc * 128:(kc + 1) * 128],
                           ln_fm[:, kc * T:(kc + 1) * T],
                           start=(kc == 0),
                           stop=(not last and kc == KC - 1)))
                mm_ev = pe_end(ps[bank]) if last else pe_end_defer()
                bank_w[bank] = mm_ev
                free_piece(g, mm_ev)
                gsl = gate_st[:, (c % 2) * T:(c % 2 + 1) * T]
                r_gst.acquire(w, "scalar", c)
                wev("scalar", mm_ev)
                sig_ev = av(lambda e, gsl=gsl, bank=bank: e.activation(
                    gsl, ps[bank][:, :], AF.Sigmoid), inc=True)
                bank_r[bank].append(sig_ev)
                wev("vector", rstdb_ev)
                wev("vector", sig_ev)
                dv(lambda e, c=c: e.tensor_mul(
                    af[:, c * T:(c + 1) * T], af[:, c * T:(c + 1) * T],
                    rstdb[:, :]))
                dv(lambda e, c=c: e.tensor_scalar_mul(
                    af[:, c * T:(c + 1) * T], af[:, c * T:(c + 1) * T],
                    nw_s[:, c:c + 1]))
                g_ev = dv2(lambda e, c=c, gsl=gsl: e.tensor_mul(
                    af[:, c * T:(c + 1) * T], af[:, c * T:(c + 1) * T], gsl))
                gated_evs[c] = g_ev
                r_gst.release(c, *g_ev)
            lnfm_free_evs.append(mm_ev)

            if stop_after == "gate":
                lnfm_free_evs.append(mm_ev)
                continue

            y_bs = {}
            for fg in range(4):
                for kp in range(4):
                    g = load_piece(P_OUT + fg * 4 + kp)
                    slot = use_piece(g)
                    p_ev = None
                    for kc in range(4):
                        K = kp * 4 + kc
                        wev("tensor", gated_evs[K])
                        for t in range(TC):
                            if kp == 0 and kc == 0:
                                bank_pre_write(t)
                            p_ev = pe(lambda e, K=K, t=t, kc=kc, slot=slot:
                                      e.matmul(
                                          ps[t][:, :],
                                          af[:, K * T + t * 128:
                                             K * T + (t + 1) * 128],
                                          slot[:, kc * 512:(kc + 1) * 512],
                                          start=(kp == 0 and kc == 0),
                                          stop=False),
                                      inc=(kc == 3 and t == TC - 1))
                    free_piece(g, p_ev)
                    if kp == 3:
                        grp_ev = pe_end(ps[TC - 1])
                        for t in range(TC):
                            bank_w[t] = grp_ev
                for t in range(TC):
                    wev("scalar", bank_w[t])
                    yev = av(lambda e, t=t, fg=fg: e.activation(
                        qy[:, t * DM + fg * 512:t * DM + (fg + 1) * 512],
                        ps[t][:, :], AF.Identity), inc=True)
                    bank_r[t].append(yev)
                    wev("vector", yev)
                    y_bs[t] = dv(lambda e, t=t, fg=fg: e.bn_stats(
                        bn6[:, t * 24 + fg * 6:t * 24 + (fg + 1) * 6],
                        qy[:, t * DM + fg * 512:t * DM + (fg + 1) * 512]),
                        inc=(fg == 3))

            lny = {}
            for tc in range(TC):
                lny[tc] = ln_stats(1, tc, y_bs[tc], reg=tc)
            yap_free = None
            new_xs_guard = []
            for tc in range(TC):
                xsl = x_sb[:, (tc % 2) * DM:(tc % 2 + 1) * DM]
                r_x.acquire(w, "scalar", tc)
                xev = dma("scalar", xsl,
                          xin[r0 + tc * 128:r0 + (tc + 1) * 128, :],
                          s[f"xld{tc % 2}"])
                sc, bi, st_ev = lny[tc]
                wev("vector", st_ev)
                wev("vector", xev)
                dv(lambda e, xsl=xsl, bi=bi: e.tensor_scalar_add(
                    xsl, xsl, bi))
                resid_ev = dv(lambda e, xsl=xsl, tc=tc, sc=sc:
                              e.scalar_tensor_tensor(
                                  xsl, qy[:, tc * DM:(tc + 1) * DM], sc, xsl,
                                  ALU.mult, ALU.add), inc=True)
                mom_free[(1, tc)] = resid_ev
                bs = None
                for q in range(4):
                    bs = dv(lambda e, q=q, xsl=xsl, tc=tc: e.bn_stats(
                        bn6[:, tc * 24 + q * 6:tc * 24 + (q + 1) * 6],
                        xsl[:, q * 512:(q + 1) * 512]), inc=(q == 3))
                sc2, bi2, st2_ev = ln_stats(2, tc, bs, reg=tc)
                lsl = ln_st[:, (tc % 2) * DM:(tc % 2 + 1) * DM]
                r_lnst.acquire(w, "scalar", tc)
                wev("scalar", st2_ev)
                ap2_ev = av(lambda e, lsl=lsl, xsl=xsl, sc2=sc2, bi2=bi2:
                            e.activation(lsl, xsl, AF.Identity, scale=sc2,
                                         bias=bi2), inc=True)
                mom_free[(2, tc)] = ap2_ev
                wev("scalar", ap2_ev)
                w.wait("scalar", s["ln_fmd"], lnfm_val)
                lev = dma("scalar", b_ln[tc * 128:(tc + 1) * 128, :], lsl,
                          s[f"lnst{tc % 2}"])
                r_lnst.release(tc, *lev)
                wev("scalar", resid_ev)
                for xg in xs_guard:
                    wev("scalar", xg)
                xsev = dma("scalar", xs[tc * 128:(tc + 1) * 128, :], xsl,
                           s[f"xs{tc % 2}"])
                r_x.release(tc, *xsev)

            for ev in lnfm_free_evs:
                wev("scalar", ev)
            lnfm_free_evs = []
            w.wait("scalar", s["lnst0"])
            w.wait("scalar", s["lnst1"])
            lnfm_base = s["ln_fmd"].n
            for c in range(KC):
                dma("scalar", ln_fm[:, c * T:(c + 1) * T],
                    b_ln[:, c * 128:(c + 1) * 128], s["ln_fmd"], transpose=True)
            lnfm_val = s["ln_fmd"].n
            lnfm_prev = lnfm_val

            if stop_after == "ln2":
                continue

            for m in range(MC_G):
                g = load_piece(P_G + m)
                bank = m % 4
                adj_ev, ast_ev = adjust_piece(j, g, P_G + m)
                slot = use_piece(g)
                wev("tensor", adj_ev)
                bank_pre_write(bank)
                last = (m == MC_G - 1)
                for kc in range(KC):
                    w.wait("tensor", s["ln_fmd"], lnfm_base + (kc + 1) * 16)
                    pe(lambda e, kc=kc, slot=slot, bank=bank, last=last:
                       e.matmul(
                           ps[bank][:, :], slot[:, kc * 128:(kc + 1) * 128],
                           ln_fm[:, kc * T:(kc + 1) * T],
                           start=(kc == 0),
                           stop=(not last and kc == KC - 1)))
                mm_ev = pe_end(ps[bank]) if last else pe_end_defer()
                bank_w[bank] = mm_ev
                free_piece_multi(g, [mm_ev, ast_ev])
                wev("scalar", mm_ev)
                gev = av(lambda e, m=m, bank=bank: e.activation(
                    gu_buf[:, m * T:(m + 1) * T], ps[bank][:, :], AF.Silu,
                    scale=sg_s[:, m:m + 1]), inc=True)
                bank_r[bank].append(gev)
            gu_rdy = {}
            for m in range(MC_G):
                g = load_piece(P_U + m)
                bank = m % 4
                adj_ev, ast_ev = adjust_piece(j, g, P_U + m)
                slot = use_piece(g)
                wev("tensor", adj_ev)
                bank_pre_write(bank)
                last = (m == MC_G - 1) or (j == 0)
                for kc in range(KC):
                    pe(lambda e, kc=kc, slot=slot, bank=bank, last=last:
                       e.matmul(
                           ps[bank][:, :], slot[:, kc * 128:(kc + 1) * 128],
                           ln_fm[:, kc * T:(kc + 1) * T],
                           start=(kc == 0),
                           stop=(not last and kc == KC - 1)))
                mm_ev = pe_end(ps[bank]) if last else pe_end_defer()
                bank_w[bank] = mm_ev
                free_piece_multi(g, [mm_ev, ast_ev])
                wev("vector", mm_ev)
                dve_guard()
                uev = dv2(lambda e, m=m, bank=bank: e.scalar_tensor_tensor(
                    gu_buf[:, m * T:(m + 1) * T], ps[bank][:, :],
                    su_s[:, m:m + 1], gu_buf[:, m * T:(m + 1) * T],
                    ALU.mult, ALU.mult))
                bank_r[bank].append(uev)
                gu_rdy[m] = uev

            f_bs = {}
            for fg in range(4):
                for kp in range(11):
                    g = load_piece(P_DOWN + fg * 11 + kp)
                    adj_ev, ast_ev = adjust_piece(j, g, P_DOWN + fg * 11 + kp)
                    slot = use_piece(g)
                    wev("tensor", adj_ev)
                    p_ev = None
                    for kc in range(4):
                        K = kp * 4 + kc
                        wev("tensor", gu_rdy[K])
                        for t in range(TC):
                            if kp == 0 and kc == 0:
                                bank_pre_write(t)
                            p_ev = pe(lambda e, K=K, t=t, kc=kc, slot=slot:
                                      e.matmul(
                                          ps[t][:, :],
                                          gu_buf[:, K * T + t * 128:
                                                 K * T + (t + 1) * 128],
                                          slot[:, kc * 512:(kc + 1) * 512],
                                          start=(kp == 0 and kc == 0),
                                          stop=False),
                                      inc=(kc == 3 and t == TC - 1))
                    free_piece_multi(g, [p_ev, ast_ev])
                    if kp == 10:
                        grp_ev = pe_end(ps[TC - 1])
                        for t in range(TC):
                            bank_w[t] = grp_ev
                for t in range(TC):
                    wev("scalar", bank_w[t])
                    fev = av(lambda e, t=t, fg=fg: e.activation(
                        ln_fm[:, t * DM + fg * 512:t * DM + (fg + 1) * 512],
                        ps[t][:, :], AF.Identity), inc=True)
                    bank_r[t].append(fev)
                    wev("vector", fev)
                    f_bs[t] = dv(lambda e, t=t, fg=fg: e.bn_stats(
                        bn6[:, t * 24 + fg * 6:t * 24 + (fg + 1) * 6],
                        ln_fm[:, t * DM + fg * 512:t * DM + (fg + 1) * 512]),
                        inc=(fg == 3))

            lnf = {}
            for tc in range(TC):
                lnf[tc] = ln_stats(3, tc, f_bs[tc], reg=tc)
            fap_evs_prev = {}
            for tc in range(TC):
                xsl = x_sb[:, (tc % 2) * DM:(tc % 2 + 1) * DM]
                r_x.acquire(w, "scalar", tc)
                w.wait("scalar", s["xs0"], s["xs0"].n)
                w.wait("scalar", s["xs1"], s["xs1"].n)
                xev = dma("scalar", xsl, xs[tc * 128:(tc + 1) * 128, :],
                          s[f"xld{tc % 2}"])
                sc3, bi3, st3_ev = lnf[tc]
                wev("scalar", st3_ev)
                wev("scalar", yap_free)
                fap_ev = av(lambda e, tc=tc, sc3=sc3, bi3=bi3: e.activation(
                    y_ap[:, :], ln_fm[:, tc * DM:(tc + 1) * DM], AF.Identity,
                    scale=sc3, bias=bi3), inc=True)
                mom_free[(3, tc)] = fap_ev
                fap_evs_prev[tc] = fap_ev
                wev("vector", fap_ev)
                wev("vector", xev)
                fr_ev = dv(lambda e, xsl=xsl: e.tensor_add(
                    xsl, xsl, y_ap[:, :]), inc=True)
                yap_free = fr_ev
                wev("scalar", fr_ev)
                oev = dma("scalar",
                          out_tm[r0 + tc * 128:r0 + (tc + 1) * 128, :], xsl,
                          s[f"out{tc % 2}"])
                r_x.release(tc, *oev)
                new_xs_guard.append(xev)
            xs_guard = new_xs_guard

        w.wait("scalar", s["out0"], s["out0"].n)
        w.wait("scalar", s["out1"], s["out1"].n)
        w.wait("gpsimd", s["pre_st0"], s["pre_st0"].n)
        w.wait("gpsimd", s["pre_st1"], s["pre_st1"].n)

        tap_src = {"ln_fm": ln_fm, "q_fm": qy, "k_fm": k_fm, "v_tm": v_tm,
                   "k_tm": k_tm, "attn": af, "y_tm": qy, "gu": gu_buf,
                   "f_tm": ln_fm, "rstdb": rstdb, "S": S_all}
        if taps:
            for nm_ in ["ln_fmd", "kv_st", "vst0", "vst1", "ktm", "vtm",
                        "misc", "pe", "act", "dvec", "pool"]:
                w.wait("gpsimd", s[nm_], s[nm_].n)
        for nm, ap in taps.items():
            src = tap_src[nm]
            dma("gpsimd", ap, src[:, 0:ap.shape[-1]], s["tapd"])
        w.wait("gpsimd", s["tapd"], s["tapd"].n)

    with nc.Block() as block:
        for name in ["sync", "scalar", "vector", "tensor", "gpsimd"]:
            def make(nm):
                def f(eng):
                    schedule(Walk(nm, eng, list(SM.values())))
                return f
            getattr(block, name)(make(name))

    ctx.close()
    return nc



def _tables():
    base = 1.0 / 2.0 ** (8.0 / H)
    rate = base ** (np.arange(H, dtype=np.float64) + 1.0)
    factor = 1.0 - LAYER_IDX / (NUM_LAYERS - 1 + 1e-5) + 1e-5
    slope = (rate * factor)[:, None]
    i = np.arange(BS, dtype=np.float64)[None, :]
    qd = np.exp(-slope * (i + 1.0))
    kd = np.exp(-slope * (BS - (i + 1.0)))
    dif = i[0][:, None] - i[0][None, :]
    ddm = np.exp(-slope[:, None] * np.maximum(dif, 0.0)) * (dif >= 0)
    qd_t = np.zeros((128, KC * BS), np.float64)
    for c in range(KC):
        for half in range(2):
            qd_t[half * 64:(half + 1) * 64, c * BS:(c + 1) * BS] = qd[2 * c + half]
    kd_t = np.zeros((128, 2 * H), np.float64)
    for jc in range(2):
        for h in range(H):
            kd_t[:, jc * H + h] = kd[h, jc * 128:(jc + 1) * 128]
    ddT = np.zeros((H, 128, 2 * BS), np.float64)
    for jc in range(2):
        ddT[:, :, jc * BS:(jc + 1) * BS] = np.transpose(
            ddm[:, :, jc * 128:(jc + 1) * 128], (0, 2, 1))
    return (np.ascontiguousarray(qd_t.astype(ml_dtypes.bfloat16)),
            np.ascontiguousarray(kd_t.astype(np.float32)),
            np.ascontiguousarray(ddT.astype(ml_dtypes.bfloat16)))


def _prep_shared(inp):
    qkv_w = np.asarray(inp["L0_attn_qkv_w"])
    idx = np.arange(3 * DM)
    part, rem = idx // DM, idx % DM
    perm = (rem // D) * (3 * D) + part * D + (rem % D)
    qkvT = qkv_w[perm].T.astype(np.float32)

    def mtile(wT):
        K_, M_ = wT.shape
        return np.ascontiguousarray(
            wT.reshape(K_ // 128, 128, M_ // 128, 128)
            .transpose(2, 1, 0, 3).reshape(M_ // 128 * 128, K_ // 128 * 128)
            .reshape(M_ // 128 * 128, K_))

    def ftile(wT):
        K_, M_ = wT.shape
        a = wT.reshape(K_ // 512, 4, 128, M_ // 512, 512)
        a = a.transpose(3, 0, 2, 1, 4)
        return np.ascontiguousarray(a.reshape(M_ // 512 * (K_ // 512) * 128,
                                              2048))

    qd_t, kd_t, ddT = _tables()
    return dict(
        qkvT=mtile(qkvT),
        gateT=mtile(np.asarray(inp["L0_attn_gate_w"]).T.astype(np.float32)),
        outT=ftile(np.asarray(inp["L0_attn_out_w"]).T.astype(np.float32)),
        wqgT=mtile(np.asarray(inp["L0_ffn_wq_gate"]).T.astype(np.int32)),
        wquT=mtile(np.asarray(inp["L0_ffn_wq_up"]).T.astype(np.int32)),
        wqdT=ftile(np.asarray(inp["L0_ffn_wq_down"]).T.astype(np.int32)),
        sg_p=np.ascontiguousarray(
            np.asarray(inp["L0_ffn_s_gate"]).astype(np.float32).reshape(MC_G, 128).T),
        su_p=np.ascontiguousarray(
            np.asarray(inp["L0_ffn_s_up"]).astype(np.float32).reshape(MC_G, 128).T),
        sd_bc=np.ascontiguousarray(np.broadcast_to(
            np.asarray(inp["L0_ffn_s_down"]).astype(np.float32)[None, :], (128, DM))),
        nw_p=np.ascontiguousarray(
            np.asarray(inp["L0_attn_norm_w"]).astype(np.float32).reshape(KC, 128).T),
        qd_t=qd_t, kd_t=kd_t, ddT_t=ddT)


def kernel(**inputs):
    x = np.asarray(inputs["L0_attn_hidden_states"])
    shared = _prep_shared(inputs)
    nc = build_program(n_tiles=8)
    in_maps = []
    for c in range(NCORE):
        m = dict(shared)
        m["xin"] = np.ascontiguousarray(
            x[2 * c:2 * c + 2].reshape(8 * T, DM).astype(np.float32))
        in_maps.append(m)
    res = run_bass_kernel_spmd(nc, in_maps, core_ids=list(range(NCORE)))
    outs = [r["out_tm"].reshape(2, 2048, DM) for r in res.results]
    return np.concatenate(outs, axis=0)
